# version 10
# baseline (speedup 1.0000x reference)
import numpy as np
import ml_dtypes
from collections import defaultdict
from contextlib import ExitStack

import concourse.bass as bass
import concourse.mybir as mybir
from concourse.bass_utils import run_bass_kernel_spmd

F32 = mybir.dt.float32
BF16 = mybir.dt.bfloat16
I8 = mybir.dt.int8
I32 = mybir.dt.int32
AF = mybir.ActivationFunctionType
ALU = mybir.AluOpType
AX = mybir.AxisListType
bfloat16 = ml_dtypes.bfloat16

D = 1024
C = 4096
T = 8192
NB = 2
H = 16
NT = 32
TS = 256
SQ32 = float(np.sqrt(32.0))
LOG127 = float(np.log(127.0))
CUT = 50.0
EPS = 1e-5
PHASES = 5
NHP = 99
APARTS = set('sciexmude')

SLOPES = [2.0 ** (-8.0 * (h + 1) / H) for h in range(H)]
DPACK = [max(CUT * SQ32 / SLOPES[4 * g + m] for m in range(4)) for g in range(4)]

def core_tile(c, k):
    b = k // 4
    pp = 8 * (k % 4) + ((c - 4 * b) % 8)
    return b, pp

def owner(b, pp):
    return (pp + 4 * b) % 8

EK = [0] * 8
FKG = [[0] * 4 for _ in range(8)]
for k in range(8):
    q = k % 4
    EK[k] = 16 * q + 14
    S0 = 2048 * q
    for g in range(4):
        jmin = max(0, int(np.ceil((S0 - 128 - DPACK[g]) / 128.0)))
        FKG[k][g] = min(jmin, EK[k])

def schedule_entries():
    ents = []
    for k in range(7, -1, -1):
        for hp in range(2):
            packs = (3, 2) if hp == 0 else (1, 0)
            f0 = min(FKG[k][g] for g in packs)
            for J in range(f0, EK[k]):
                for g in packs:
                    if FKG[k][g] <= J:
                        ents.append(dict(kind="far", k=k, hp=hp, J=J, g=g))
            for g in packs:
                for sb_ in range(2):
                    ents.append(dict(kind="diag", k=k, hp=hp, g=g, sb=sb_))
    return ents

ENTRIES = schedule_entries()
NENT = len(ENTRIES)

ENGS = ("PE", "ACT", "DVE", "GP", "SYNC")


class Sched:
    def __init__(self):
        self.ops = []
        self.last_w = {}
        self.readers = {}

    def op(self, eng, fn, r=(), w=(), dma=False, cc=None, tag="", fence=False):
        i = len(self.ops)
        deps = set()
        for kk in r:
            if kk in self.last_w:
                deps.add(self.last_w[kk])
        for kk in w:
            if kk in self.last_w:
                deps.add(self.last_w[kk])
            for rd in self.readers.get(kk, ()):
                deps.add(rd)
        self.ops.append(dict(i=i, eng=eng, fn=fn, deps=deps, dma=dma, cc=cc,
                             need=False, tag=tag, fence=fence))
        for kk in r:
            self.readers.setdefault(kk, []).append(i)
        for kk in w:
            self.last_w[kk] = i
            self.readers[kk] = []
        return i

    def finalize(self):
        for o in self.ops:
            real = []
            for d in sorted(o["deps"]):
                dop = self.ops[d]
                if (dop["eng"] == o["eng"] and not dop["dma"]
                        and dop["cc"] is None and not dop["fence"]):
                    continue
                real.append(d)
                dop["need"] = True
            o["wdeps"] = real
        cnt = defaultdict(int)
        for o in self.ops:
            if o["cc"] is not None:
                o["ms"] = (o["cc"], None, None)
                continue
            if o["dma"]:
                o["need"] = True
            if o["need"]:
                s = ("d" if o["dma"] else "s") + o["eng"]
                step = 16 if o["dma"] else 1
                cnt[s] += step
                o["ms"] = (s, cnt[s], step)
        emis = {e: [] for e in ENGS}
        wm = {e: defaultdict(int) for e in ENGS}
        for o in self.ops:
            ws = []
            for d in o["wdeps"]:
                dop = self.ops[d]
                s, v, _ = dop["ms"]
                if v is None:
                    v = 1
                if wm[o["eng"]][s] < v:
                    ws.append((s, v))
                    wm[o["eng"]][s] = v
            emis[o["eng"]].append((ws, o["fn"], o.get("ms"), o["tag"]))
        return emis


def build_program():
    nc = bass.Bass("TRN2", target_bir_lowering=False, detect_race_conditions=False)
    sch = Sched()
    st = ExitStack()

    xd = nc.dram_tensor("x", [8, D, 260], F32, kind="ExternalInput")
    w1Td = nc.dram_tensor("w1T", [D, C], F32, kind="ExternalInput")
    w2Td = nc.dram_tensor("w2T", [C, D], F32, kind="ExternalInput")
    convd = nc.dram_tensor("convw", [128, 128], F32, kind="ExternalInput")
    wqd = nc.dram_tensor("Wq", [D, 512], F32, kind="ExternalInput")
    wkd = nc.dram_tensor("Wk", [D, 128], F32, kind="ExternalInput")
    wvd = nc.dram_tensor("Wv", [D, 256], F32, kind="ExternalInput")
    wpd = nc.dram_tensor("Wp", [D, D], F32, kind="ExternalInput")
    bpd = nc.dram_tensor("bp", [128, 8], F32, kind="ExternalInput")
    ctabd = nc.dram_tensor("ctab", [4, NENT * 256], BF16, kind="ExternalInput")
    seld = nc.dram_tensor("sel", [48, 16], F32, kind="ExternalInput")
    edged = nc.dram_tensor("edge", [2, 128, 258], F32, kind="ExternalInput")
    trid = nc.dram_tensor("tri", [128, 128], BF16, kind="ExternalInput")
    selrowd = nc.dram_tensor("selrow", [128, 256], F32, kind="ExternalInput")
    idfd = nc.dram_tensor("identf", [128, 128], F32, kind="ExternalInput")

    outd = nc.dram_tensor("xout", [8, D, 256], F32, kind="ExternalOutput")
    dbg1 = nc.dram_tensor("dbg1", [8, D, 260], F32, kind="ExternalOutput")
    dbg2 = nc.dram_tensor("dbg2", [8, D, 260], F32, kind="ExternalOutput")
    dbg3 = nc.dram_tensor("dbg3", [12, 128, 260], F32, kind="ExternalOutput")
    dbg4 = nc.dram_tensor("dbg4", [8, 128, 512], F32, kind="ExternalOutput")
    dbg5 = nc.dram_tensor("dbg5", [4, 128, 1024], F32, kind="ExternalOutput")

    w1qd = nc.dram_tensor("w1q", [D, C], BF16)
    w2qd = nc.dram_tensor("w2q", [C, D], BF16)
    kcond = nc.dram_tensor("kcon", [128, 2048], BF16)
    vcond = nc.dram_tensor("vcon", [2048, 256], BF16)
    kgatd = nc.dram_tensor("kgat", [1024, 2048], BF16)
    vgatd = nc.dram_tensor("vgat", [16384, 256], BF16)
    halind = nc.dram_tensor("halin", [16, 1024], F32)
    halgd = nc.dram_tensor("halg", [128, 1024], F32)

    def sb(name, shape, dt):
        return st.enter_context(nc.sbuf_tensor(name, shape, dt))

    def ps(name, shape):
        return st.enter_context(nc.psum_tensor(name, shape, F32))

    XT = [sb(f"XT{t}", [128, 8 * 260], F32) for t in range(8)]
    SL = sb("SL", [128, 8256], F32)
    KH = [sb("KH0", [128, 32 * 258], BF16)]
    XQ = sb("XQ", [128, 8 * 260], BF16)
    W1S = [sb(f"W1S{i}", [128, 1024], BF16) for i in range(2)]
    W2S = [sb(f"W2S{i}", [128, 4096], BF16) for i in range(2)]
    ACC = [sb(f"ACC{i}", [128, 260], F32) for i in range(3)]
    H1S = [sb(f"H1S{i}", [128, 260], F32) for i in range(2)]
    RWS = [sb(f"RWS{i}", [128, 260], F32) for i in range(3)]
    BC = [sb(f"BC{i}", [128, 260], F32) for i in range(3)]
    CW = sb("CW", [128, 128], F32)
    QT = [sb(f"QT{t}", [128, 1024], BF16) for t in range(8)]
    KL = [sb(f"KL{t}", [128, 256], BF16) for t in range(8)]
    VL = [sb(f"VL{t}", [128, 512], BF16) for t in range(8)]
    KB = [sb(f"KB{i}", [128, 128], BF16) for i in range(2)]
    VB = [sb(f"VB{i}", [128, 256], BF16) for i in range(2)]
    ET = [sb(f"ET{i}", [128, 1024], BF16) for i in range(2)]
    CRING = sb("CRING", [128, 2048], BF16)
    CTXN = sb("CTXN", [128, 2048], BF16)
    RECIP = sb("RECIP", [128, 512], F32)
    RSB = [sb(f"RSB{i}", [128, 256], F32) for i in range(2)]
    WPS = [sb(f"WPS{i}", [128, 1024], BF16) for i in range(2)]
    TRI = sb("TRI", [128, 128], BF16)
    SELR = sb("SELR", [128, 256], F32)
    IDF = sb("IDF", [128, 128], F32)
    ONESB = sb("ONESB", [128, 256], BF16)
    ONESF = sb("ONESF", [128, 1], F32)
    CONSTS = sb("CONSTS", [128, 4], F32)
    ONE1F = sb("ONE1F", [1, 128], F32)
    EDG = [sb(f"EDG{i}", [128, 258], F32) for i in range(2)]
    BP = sb("BP", [128, 8], F32)
    MH = sb("MH", [128, 260], F32)
    TCOL = sb("TCOL", [128, 4], F32)
    SELC = sb("SELC", [48, 16], F32)
    HSTG = sb("HSTG", [128, 16], F32)
    HALT = SL[0:16, 0:1024]
    CANDS_ = (SL, 1024)
    WSC = sb("WSC", [128, 2], F32)
    MWACC = sb("MWACC", [128, 1], F32)
    WI8 = sb("WI8", [128, 2048], I8)

    SC = ps("SC", [128, 2048])
    CT0 = ps("CT0", [128, 512])
    CT1 = ps("CT1", [128, 512])
    SMS = ps("SMS", [128, 512])
    SP2 = ps("SP2", [128, 512])
    STAT = SP2

    class _View:
        def __init__(self, h, off):
            self.h, self.off = h, off
        def __getitem__(self, idx):
            p, f = idx
            return self.h[p, slice(self.off + (f.start or 0), self.off + f.stop)]
    FA = _View(SC, 0)
    FB = _View(SC, 512)

    semh = {}
    for s in ("sPE", "sACT", "sDVE", "sGP", "sSYNC",
              "dPE", "dACT", "dDVE", "dGP", "dSYNC", "ccKV", "ccKV2", "ccHAL"):
        semh[s] = st.enter_context(nc.semaphore(s))

    def OP(eng, fn, r=(), w=(), dma=False, cc=None, tag="", fence=False):
        sch.op(eng, fn, r=r, w=w, dma=dma, cc=cc, tag=tag, fence=fence)

    def DMA(eng, out_ap, in_ap, r=(), w=(), tag=""):
        OP(eng, lambda e, o=out_ap, i=in_ap: e.dma_start(out=o, in_=i),
           r=r, w=w, dma=True, tag=tag)

    def MM(out_ap, lhs_ap, rhs_ap, start, stop, tp=None, r=(), w=(), tag=""):
        OP("PE", lambda e, o=out_ap, l=lhs_ap, rr=rhs_ap, s1=start, s2=stop, t=tp:
           e.matmul(o, l, rr, start=s1, stop=s2, tile_position=t, skip_group_check=True),
           r=r, w=w, tag=tag)

    def RS(i, w=260):
        t = RWS[i // 4]
        b = 32 * (i % 4)
        return t[b:b + 1, 0:w]

    RKEY = [f"row{i}" for i in range(12)]

    DMA("SYNC", TRI[:, :], trid[:, :], w=["TRI"])
    DMA("SYNC", IDF[:, :], idfd[:, :], w=["IDF"])
    DMA("SYNC", SELR[:, :], selrowd[:, :], w=["SELR"])
    DMA("SYNC", BP[:, :], bpd[:, :], w=["BP"])
    DMA("SYNC", CW[:, :], convd[:, :], w=["CW"])
    DMA("SYNC", SELC[:, :], seld[:, :], w=["SELC"])
    for i in range(2):
        DMA("SYNC", EDG[i][:, :], edged[i, :, :], w=[f"EDG{i}"])
    OP("GP", lambda e: e.memset(ONESB[:, :], 1.0), w=["ONESB"])
    OP("GP", lambda e: e.memset(RECIP[:, :], 1.0), w=["recip"])
    OP("GP", lambda e: e.memset(ONESF[:, :], 1.0), w=["ONESF"])
    OP("GP", lambda e: e.memset(CONSTS[:, 0:1], EPS), w=["CONSTS"])
    OP("GP", lambda e: e.memset(CONSTS[:, 1:2], LOG127), w=["CONSTS"])
    OP("GP", lambda e: e.memset(ONE1F[:, :], 1.0), w=["ONE1F"])

    def wprep(wTd, wqdst, nslices, width, wcol, key):
        for s_ in range(nslices):
            DMA("SYNC", SL[:, 0:width], wTd[128 * s_:128 * (s_ + 1), :],
                w=["sl"], tag=f"wld_{key}_{s_}")
            OP("DVE", lambda e, w_=width: e.reduce_sum(TCOL[:, 0:1], SL[:, 0:w_],
               axis=AX.X, apply_absolute_value=True),
               r=["sl"], w=["tcol0"], fence=True)
            if s_ == 0:
                OP("DVE", lambda e: e.tensor_copy(MWACC[:, :], TCOL[:, 0:1]),
                   r=["tcol0"], w=["mwacc"])
            else:
                OP("DVE", lambda e: e.tensor_add(MWACC[:, :], MWACC[:, :], TCOL[:, 0:1]),
                   r=["tcol0", "mwacc"], w=["mwacc"])
        MM(STAT[0:1, 0:1], ONESF[:, :], MWACC[:, :], True, True,
           r=["mwacc", "ONESF"], w=["pSP2"], tag=f"wsum_{key}")
        n_el = nslices * 128 * width
        OP("ACT", lambda e, n=n_el: e.activation(RS(0)[:, 0:1], STAT[0:1, 0:1],
           AF.Copy, bias=0.0, scale=1.0 / n), r=["pSP2"], w=["row0"])
        OP("DVE", lambda e: e.tensor_scalar(out=RS(0)[:, 0:1], in0=RS(0)[:, 0:1],
           scalar1=1e-5, scalar2=None, op0=ALU.max), r=["row0"], w=["row0"])
        OP("DVE", lambda e: e.reciprocal(RS(4)[:, 0:1], RS(0)[:, 0:1]),
           r=["row0"], w=["row4"], fence=True)
        MM(SP2[:, 0:1], ONE1F[:, :], RS(4)[:, 0:1], True, True,
           r=["row4", "ONE1F"], w=["pSP2"])
        OP("ACT", lambda e, wc=wcol: e.copy(wc, SP2[:, 0:1]),
           r=["pSP2"], w=[key + "wsc"])
        cidx = 2 if key == "w1" else 3
        OP("DVE", lambda e, ci=cidx: e.tensor_scalar(out=CONSTS[0:1, ci:ci + 1],
           in0=RS(0)[:, 0:1], scalar1=1.0 / 127.0, scalar2=None, op0=ALU.mult),
           r=["row0"], w=["CONSTS"])
        for s_ in range(nslices):
            DMA("SYNC", SL[:, 0:width], wTd[128 * s_:128 * (s_ + 1), :],
                w=["sl"], tag=f"wld2_{key}_{s_}")
            for h0 in range(0, width, 2048):
                hw = min(2048, width - h0)
                OP("DVE", lambda e, wc=wcol, o=h0, hw_=hw: e.tensor_scalar(
                   out=SL[:, 4160:4160 + hw_],
                   in0=SL[:, o:o + hw_], scalar1=wc, scalar2=1.4995,
                   op0=ALU.mult, op1=ALU.min), r=["sl", key + "wsc"], w=["sl"])
                OP("DVE", lambda e, hw_=hw: e.tensor_scalar(out=WI8[:, 0:hw_],
                   in0=SL[:, 4160:4160 + hw_], scalar1=-1.4995, scalar2=None,
                   op0=ALU.max), r=["sl"], w=["wi8"])
                OP("DVE", lambda e, hw_=hw: e.tensor_copy(XQ[:, 0:hw_], WI8[:, 0:hw_]),
                   r=["wi8"], w=["xq"])
                DMA("SYNC", wqdst[128 * s_:128 * (s_ + 1), h0:h0 + hw],
                    XQ[:, 0:hw], r=["xq"], tag=f"wst_{key}_{s_}_{h0}")

    wprep(w1Td, w1qd, 8, C, WSC[:, 0:1], "w1")
    wprep(w2Td, w2qd, 32, D, WSC[:, 1:2], "w2")


    def bcast_build(row_ap, dst_bc, wdt, rkeys, bckey, tag=""):
        MM(SP2[:, 0:wdt], ONE1F[:, :], row_ap, True, True,
           r=list(rkeys) + ["ONE1F"], w=["pSP2"], tag="bc" + tag)
        OP("ACT", lambda e, d=dst_bc, w_=wdt: e.copy(d[:, 0:w_], SP2[:, 0:w_]),
           r=["pSP2"], w=[bckey], tag="bce" + tag)

    def dance(M_t, mkey, wdt, outrow_i, tag=""):
        OP("DVE", lambda e, w_=wdt, M=M_t: e.tensor_copy(ACC[0][0:64, 0:w_],
           M[64:128, 0:w_]), r=[mkey], w=["acc0"], tag="dc0" + tag)
        OP("DVE", lambda e, w_=wdt, M=M_t: e.tensor_tensor(out=M[0:64, 0:w_],
           in0=M[0:64, 0:w_], in1=ACC[0][0:64, 0:w_], op=ALU.max),
           r=[mkey, "acc0"], w=[mkey], tag="dc1" + tag)
        OP("DVE", lambda e, w_=wdt, M=M_t: e.tensor_copy(ACC[0][0:32, 0:w_],
           M[32:64, 0:w_]), r=[mkey], w=["acc0"], tag="dc0b" + tag)
        OP("DVE", lambda e, w_=wdt, M=M_t: e.tensor_tensor(out=M[0:32, 0:w_],
           in0=M[0:32, 0:w_], in1=ACC[0][0:32, 0:w_], op=ALU.max),
           r=[mkey, "acc0"], w=[mkey], tag="dc2" + tag)
        off = 0
        while off < wdt:
            ln = min(128, wdt - off)
            OP("PE", lambda e, o=off, l=ln, M=M_t: e.transpose(SP2[0:l, 0:32],
               M[0:32, o:o + l], IDF[0:32, 0:32]),
               r=[mkey, "IDF"], w=["pSP2"], tag="dct" + tag)
            OP("DVE", lambda e, l=ln: e.reduce_max(TCOL[0:l, 1:2], SP2[0:l, 0:32],
               axis=AX.X), r=["pSP2"], w=["tcol1"], tag="dcr" + tag, fence=True)
            OP("PE", lambda e, l=ln: e.transpose(SP2[0:1, 32:32 + l],
               TCOL[0:l, 1:2], IDF[0:l, 0:l]),
               r=["tcol1", "IDF"], w=["pSP2"], tag="dct2" + tag)
            OP("ACT", lambda e, o=off, l=ln, ri=outrow_i: e.copy(
               RS(ri)[:, o:o + l], SP2[0:1, 32:32 + l]),
               r=["pSP2"], w=[RKEY[outrow_i]], tag="dce" + tag)
            off += 128

    def ffn(t, Wd, call, fpar):
        xo = 0 if call == 1 else 1
        Wo = Wd - 2
        xk = f"X{t}"
        for ch in range(8):
            xs = XT[t][:, 260 * ch + xo: 260 * ch + xo + Wd]
            MM(STAT[64:65, 0:Wd], ONESF[:, :], xs, ch == 0, ch == 7, tp=(0, 64),
               r=[xk, "ONESF"], w=["pSP2"], tag=f"st{call}_{t}_{ch}")
            OP("ACT", lambda e, x_=xs, p=ch % 2, w_=Wd: e.activation(
               ACC[p][:, 0:w_], x_, AF.Square), r=[xk], w=[f"acc{ch % 2}"],
               tag=f"sq{call}_{t}_{ch}")
            MM(STAT[96:97, 0:Wd], ONESF[:, :], ACC[ch % 2][:, 0:Wd], ch == 0, ch == 7,
               tp=(0, 96), r=[f"acc{ch % 2}", "ONESF"], w=["pSP2"],
               tag=f"sx{call}_{t}_{ch}")
        OP("ACT", lambda e, w_=Wd: e.activation(RS(0)[:, 0:w_], STAT[64:65, 0:w_],
           AF.Copy, bias=0.0, scale=1.0 / 1024.0), r=["pSP2"], w=["row0"])
        OP("ACT", lambda e, w_=Wd: e.activation(RS(4)[:, 0:w_], STAT[96:97, 0:w_],
           AF.Copy, bias=0.0, scale=1.0 / 1024.0), r=["pSP2"], w=["row4"])
        OP("DVE", lambda e, w_=Wd: e.tensor_tensor(out=RS(8)[:, 0:w_],
           in0=RS(0)[:, 0:w_], in1=RS(0)[:, 0:w_], op=ALU.mult),
           r=["row0"], w=["row8"])
        OP("DVE", lambda e, w_=Wd: e.tensor_tensor(out=RS(4)[:, 0:w_],
           in0=RS(4)[:, 0:w_], in1=RS(8)[:, 0:w_], op=ALU.subtract),
           r=["row4", "row8"], w=["row4"])
        OP("ACT", lambda e, w_=Wd: e.activation(RS(4)[:, 0:w_], RS(4)[:, 0:w_],
           AF.Ln, bias=CONSTS[0:1, 0:1], scale=1.0), r=["row4", "CONSTS"], w=["row4"])
        OP("ACT", lambda e, w_=Wd: e.activation(RS(4)[:, 0:w_], RS(4)[:, 0:w_],
           AF.Exp, bias=0.0, scale=-0.5), r=["row4"], w=["row4"])
        bcast_build(RS(0)[:, 0:Wd], BC[0], Wd, ["row0"], "bc0", tag=f"mu{call}{t}")
        for ch in range(8):
            xs = XT[t][:, 260 * ch + xo: 260 * ch + xo + Wd]
            OP("DVE", lambda e, x_=xs, w_=Wd: e.tensor_tensor(
               out=ACC[2][:, 0:w_], in0=x_, in1=BC[0][:, 0:w_], op=ALU.subtract),
               r=[xk, "bc0"], w=["acc2"], tag=f"tt{call}_{t}_{ch}")
            if ch == 0:
                OP("DVE", lambda e, w_=Wd: e.tensor_scalar(
                   out=MH[:, 0:w_].bitcast(I32), in0=ACC[2][:, 0:w_].bitcast(I32),
                   scalar1=0x7FFFFFFF, scalar2=None, op0=ALU.bitwise_and),
                   r=["acc2"], w=["mh"])
            else:
                OP("DVE", lambda e, w_=Wd: e.tensor_scalar(
                   out=ACC[1][:, 0:w_].bitcast(I32), in0=ACC[2][:, 0:w_].bitcast(I32),
                   scalar1=0x7FFFFFFF, scalar2=None, op0=ALU.bitwise_and),
                   r=["acc2"], w=["acc1"])
                OP("DVE", lambda e, w_=Wd: e.tensor_tensor(out=MH[:, 0:w_],
                   in0=MH[:, 0:w_], in1=ACC[1][:, 0:w_], op=ALU.max),
                   r=["acc1", "mh"], w=["mh"])
        dance(MH, "mh", Wd, 0, tag=f"d{call}{t}")
        OP("DVE", lambda e, w_=Wd: e.tensor_tensor(out=RS(8)[:, 0:w_],
           in0=RS(4)[:, 0:w_], in1=RS(0)[:, 0:w_], op=ALU.mult),
           r=["row4", "row0"], w=["row8"])
        OP("DVE", lambda e, w_=Wd: e.tensor_scalar(out=RS(8)[:, 0:w_],
           in0=RS(8)[:, 0:w_], scalar1=1e-5, scalar2=None, op0=ALU.max),
           r=["row8"], w=["row8"])
        OP("ACT", lambda e, w_=Wd: e.activation(ACC[0][0:1, 0:w_], RS(8)[:, 0:w_],
           AF.Ln), r=["row8"], w=["acc0"])
        OP("ACT", lambda e, w_=Wd: e.activation(ACC[0][0:1, 0:w_], ACC[0][0:1, 0:w_],
           AF.Exp, bias=CONSTS[0:1, 1:2], scale=-1.0), r=["acc0", "CONSTS"], w=["acc0"])
        OP("DVE", lambda e, w_=Wd: e.tensor_tensor(out=ACC[0][0:1, 0:w_],
           in0=ACC[0][0:1, 0:w_], in1=RS(4)[:, 0:w_], op=ALU.mult),
           r=["acc0", "row4"], w=["acc0"])
        bcast_build(ACC[0][0:1, 0:Wd], BC[1], Wd, ["acc0"], "bc1", tag=f"kx{call}{t}")
        OP("DVE", lambda e, w_=Wd: e.tensor_scalar(out=RS(8)[:, 0:w_],
           in0=RS(8)[:, 0:w_], scalar1=CONSTS[0:1, 2:3], scalar2=None,
           op0=ALU.mult), r=["row8", "CONSTS"], w=["row8"], tag="beta")
        if call == 3:
            ei_, eb_ = (t // 4), 32 * (t % 4)
            OP("DVE", lambda e, i_=ei_, b_=eb_, w_=Wd: e.tensor_copy(
               ACC[0][0:1, 0:w_], EDG[i_][b_:b_ + 1, 0:w_]),
               r=[f"EDG{ei_}"], w=["acc0"])
            OP("DVE", lambda e, w_=Wd: e.tensor_tensor(
               out=RS(8)[:, 0:w_], in0=RS(8)[:, 0:w_],
               in1=ACC[0][0:1, 0:w_], op=ALU.mult),
               r=["row8", "acc0"], w=["row8"])
        bcast_build(RS(8)[:, 0:Wd], BC[2], Wd, ["row8"], "bc2", tag=f"bt{call}{t}")
        if call == 1 and t == 0:
            DMA("GP", dbg3[11, 0:1, 0:4], CONSTS[0:1, :], r=["CONSTS"], tag="dbg3_11")
            DMA("GP", dbg3[0, :, :], BC[0][:, 0:260], r=["bc0"], tag="dbg3_0")
            DMA("GP", dbg3[1, :, :], BC[1][:, 0:260], r=["bc1"], tag="dbg3_1")
            DMA("GP", dbg3[2, :, :], BC[2][:, 0:260], r=["bc2"], tag="dbg3_2")
        for ch in range(8):
            xs = XT[t][:, 260 * ch + xo: 260 * ch + xo + Wd]
            OP("DVE", lambda e, x_=xs, w_=Wd: e.tensor_tensor(out=ACC[2][:, 0:w_],
               in0=x_, in1=BC[0][:, 0:w_], op=ALU.subtract),
               r=[xk, "bc0"], w=["acc2"])
            OP("DVE", lambda e, w_=Wd: e.tensor_tensor(out=WI8[:, 0:w_],
               in0=ACC[2][:, 0:w_], in1=BC[1][:, 0:w_], op=ALU.mult),
               r=["acc2", "bc1"], w=["wi8"])
            OP("DVE", lambda e, ch_=ch, w_=Wd: e.tensor_copy(
               XQ[:, 260 * ch_: 260 * ch_ + w_], WI8[:, 0:w_]),
               r=["wi8"], w=["xq"], tag=f"xq{call}_{t}_{ch}")
        for cc in range(32):
            par = cc % 2
            DMA("SYNC", W1S[par][:, :].rearrange("p (k c) -> p k c", c=128),
                w1qd[:, 128 * cc:128 * (cc + 1)].rearrange("(k p) c -> p k c", p=128),
                r=[], w=[f"w1s{par}"], tag=f"w1l{call}_{t}_{cc}")
            PSM = FA if (fpar + cc) % 2 == 0 else FB
            pk = "pFA" if (fpar + cc) % 2 == 0 else "pFB"
            for k8 in range(8):
                MM(PSM[:, 0:Wd], W1S[par][:, 128 * k8:128 * (k8 + 1)],
                   XQ[:, 260 * k8: 260 * k8 + Wd], k8 == 0, k8 == 7,
                   r=[f"w1s{par}", "xq"], w=[pk, "pSC"], tag=f"m1{call}_{t}_{cc}_{k8}")
            hp_ = cc % 2
            OP("DVE", lambda e, P=PSM, h_=hp_, w_=Wd: e.tensor_tensor(
               out=H1S[h_][:, 0:w_], in0=P[:, 0:w_], in1=BC[2][:, 0:w_],
               op=ALU.mult), r=[pk, "bc2"], w=[f"h1s{hp_}"],
               tag=f"bp_{call}_{t}_{cc}")
            OP("ACT", lambda e, h_=hp_, c_=cc, w_=Wo: e.activation(
               ACC[0][:, 0:w_], H1S[h_][:, 1:1 + w_], AF.Copy,
               scale=CW[:, 32 + c_:33 + c_]),
               r=[f"h1s{hp_}", "CW"], w=["acc0"], tag=f"cv1_{call}_{t}_{cc}")
            OP("DVE", lambda e, h_=hp_, c_=cc, w_=Wo: e.scalar_tensor_tensor(
               out=ACC[1][:, 0:w_], in0=H1S[h_][:, 0:w_], scalar=CW[:, c_:c_ + 1],
               in1=ACC[0][:, 0:w_], op0=ALU.mult, op1=ALU.add),
               r=[f"h1s{hp_}", "CW", "acc0"], w=["acc1"], tag=f"cv2_{call}_{t}_{cc}")
            OP("DVE", lambda e, h_=hp_, c_=cc, w_=Wo: e.scalar_tensor_tensor(
               out=ACC[0][:, 0:w_], in0=H1S[h_][:, 2:2 + w_], scalar=CW[:, 64 + c_:65 + c_],
               in1=ACC[1][:, 0:w_], op0=ALU.mult, op1=ALU.add),
               r=[f"h1s{hp_}", "CW", "acc1"], w=["acc0"], tag=f"cv3_{call}_{t}_{cc}")
            if call == 1 and t == 0 and cc == 0:
                DMA("GP", dbg3[3, :, :], XQ[:, 0:260], r=["xq"], tag="dbg3_3")
                DMA("GP", dbg3[4, :, :], H1S[0][:, 0:260], r=["h1s0"], tag="dbg3_4")
                DMA("GP", dbg3[5, :, 0:258], ACC[0][:, 0:258], r=["acc0"], tag="dbg3_5")
            OP("ACT", lambda e, c_=cc, w_=Wo: e.activation(
               SL[:, 258 * c_: 258 * c_ + w_], ACC[0][:, 0:w_], AF.Silu,
               bias=CW[:, 96 + c_:97 + c_], scale=1.0),
               r=["acc0", "CW"], w=["sl"], tag=f"si_{call}_{t}_{cc}")
            if call == 1 and t == 0 and cc == 0:
                DMA("GP", dbg3[6, :, 0:258], SL[:, 0:258], r=["sl"], tag="dbg3_6")
            if cc == 0:
                OP("DVE", lambda e, c_=cc, w_=Wo: e.tensor_scalar(
                   out=MH[:, 0:w_].bitcast(I32),
                   in0=SL[:, 258 * c_:258 * c_ + w_].bitcast(I32),
                   scalar1=0x7FFFFFFF, scalar2=None, op0=ALU.bitwise_and),
                   r=["sl"], w=["mh"])
            else:
                OP("DVE", lambda e, c_=cc, w_=Wo: e.tensor_scalar(
                   out=ACC[1][:, 0:w_].bitcast(I32),
                   in0=SL[:, 258 * c_:258 * c_ + w_].bitcast(I32),
                   scalar1=0x7FFFFFFF, scalar2=None, op0=ALU.bitwise_and),
                   r=["sl"], w=["acc1"])
                OP("DVE", lambda e, w_=Wo: e.tensor_tensor(out=MH[:, 0:w_],
                   in0=MH[:, 0:w_], in1=ACC[1][:, 0:w_], op=ALU.max),
                   r=["acc1", "mh"], w=["mh"])
        dance(MH, "mh", Wo, 0, tag=f"d2{call}{t}")
        OP("DVE", lambda e, w_=Wo: e.tensor_scalar(out=RS(0)[:, 0:w_],
           in0=RS(0)[:, 0:w_], scalar1=1e-5, scalar2=None, op0=ALU.max),
           r=["row0"], w=["row0"])
        OP("ACT", lambda e, w_=Wo: e.activation(RS(4)[:, 0:w_], RS(0)[:, 0:w_],
           AF.Ln), r=["row0"], w=["row4"])
        OP("ACT", lambda e, w_=Wo: e.activation(RS(4)[:, 0:w_], RS(4)[:, 0:w_],
           AF.Exp, bias=CONSTS[0:1, 1:2], scale=-1.0), r=["row4", "CONSTS"], w=["row4"])
        bcast_build(RS(4)[:, 0:Wo], BC[1], Wo, ["row4"], "bc1", tag=f"s2{call}{t}")
        OP("DVE", lambda e, w_=Wo: e.tensor_scalar(out=RS(8)[:, 0:w_],
           in0=RS(0)[:, 0:w_], scalar1=CONSTS[0:1, 3:4], scalar2=0.5,
           op0=ALU.mult, op1=ALU.mult), r=["row0", "CONSTS"], w=["row8"])
        if call == 1 and t == 0:
            DMA("GP", dbg3[7, :, 0:258], BC[1][:, 0:258], r=["bc1"], tag="dbg3_7")
            DMA("GP", dbg3[8, 0:1, 0:258], RS(8)[:, 0:258], r=["row8"], tag="dbg3_8")
            DMA("GP", dbg3[10, 0:1, 0:258], RS(0)[:, 0:258], r=["row0"], tag="dbg3_10")
        tb = 0
        for cc in range(32):
            OP("DVE", lambda e, c_=cc, w_=Wo: e.tensor_tensor(out=WI8[:, 0:w_],
               in0=SL[:, 258 * c_:258 * c_ + w_], in1=BC[1][:, 0:w_], op=ALU.mult),
               r=["sl", "bc1"], w=["wi8"], tag=f"q2a_{call}_{t}_{cc}")
            OP("DVE", lambda e, c_=cc, tb_=tb, w_=Wo: e.tensor_copy(
               KH[tb_][:, 258 * c_:258 * c_ + w_], WI8[:, 0:w_]),
               r=["wi8"], w=[f"kh{tb}"], tag=f"q2b_{call}_{t}_{cc}")
            if call == 1 and t == 0 and cc == 0:
                DMA("GP", dbg3[9, :, 0:258], KH[0][:, 0:258], r=["kh0"], tag="dbg3_9")


    def ffn_mm2(t, Wd, call):
        xo = 0 if call == 1 else 1
        Wo = Wd - 2
        bcast_build(RS(8)[:, 0:Wo], BC[0], Wo, ["row8"], "bcg0", tag=f"g{call}{t}")
        for d2 in range(8):
            par = d2 % 2
            DMA("SYNC", W2S[par][:, :].rearrange("p (k c) -> p k c", c=128),
                w2qd[:, 128 * d2:128 * (d2 + 1)].rearrange("(k p) c -> p k c", p=128),
                w=[f"w2s{par}"], tag=f"w2l{call}_{t}_{d2}")
            PSM = FA if d2 % 2 == 0 else FB
            pk = "pFA" if d2 % 2 == 0 else "pFB"
            for cc in range(32):
                MM(PSM[:, 0:Wo], W2S[par][:, 128 * cc:128 * (cc + 1)],
                   KH[0][:, 258 * cc:258 * cc + Wo], cc == 0, cc == 31,
                   r=[f"w2s{par}", "kh0"], w=[pk, "pSC"], tag=f"m2{call}_{t}_{d2}_{cc}")
            OP("DVE", lambda e, P=PSM, w_=Wo: e.tensor_tensor(
               out=ACC[2][:, 0:w_], in0=P[:, 0:w_], in1=BC[0][:, 0:w_],
               op=ALU.mult), r=[pk, "bcg0"], w=["acc2"],
               tag=f"ev1_{call}_{t}_{d2}")
            xs = XT[t][:, 260 * d2 + xo + 1: 260 * d2 + xo + 1 + Wo]
            OP("DVE", lambda e, x_=xs, w_=Wo: e.tensor_tensor(
               out=x_, in0=x_, in1=ACC[2][:, 0:w_], op=ALU.add),
               r=["acc2", f"X{t}"], w=[f"X{t}"], tag=f"ev2_{call}_{t}_{d2}")

    for t in range(8):
        for ch in range(8):
            DMA("SYNC", XT[t][:, 260 * ch:260 * (ch + 1)], xd[t, 128 * ch:128 * (ch + 1), :],
                w=[f"X{t}"], tag=f"xl{t}_{ch}")
    for t in range(8 if PHASES >= 1 else 0):
        ffn(t, 260, 1, 0)
        ffn_mm2(t, 260, 1)
        for ch in range(8):
            DMA("GP", dbg1[t, 128 * ch:128 * (ch + 1), :],
                XT[t][:, 260 * ch:260 * (ch + 1)], r=[f"X{t}"], tag=f"dbg1_{t}_{ch}")

    def load_cast(dst_ap, src_ap, wdt, dkey, tag):
        DMA("GP", dst_ap, src_ap, w=[dkey], tag="lc" + tag)

    for k8 in range(8):
        load_cast(W2S[0][:, 512 * k8:512 * (k8 + 1)], wqd[128 * k8:128 * (k8 + 1), :],
                  512, "w2s0", f"wq{k8}")
        load_cast(W2S[1][:, 128 * k8:128 * (k8 + 1)], wkd[128 * k8:128 * (k8 + 1), :],
                  128, "w2s1", f"wk{k8}")
        load_cast(W2S[1][:, 1024 + 256 * k8:1024 + 256 * (k8 + 1)],
                  wvd[128 * k8:128 * (k8 + 1), :], 256, "w2s1", f"wv{k8}")

    for t in range(8 if PHASES >= 2 else 0):
        xk = f"X{t}"
        for ch in range(8):
            xs = XT[t][:, 260 * ch + 1: 260 * ch + 259]
            MM(STAT[64:65, 0:258], ONESF[:, :], xs, ch == 0, ch == 7, tp=(0, 64),
               r=[xk, "ONESF"], w=["pSP2"], tag=f"l2s{t}_{ch}")
            OP("ACT", lambda e, x_=xs, p=ch % 2: e.activation(
               ACC[p][:, 0:258], x_, AF.Square), r=[xk], w=[f"acc{ch % 2}"])
            MM(STAT[96:97, 0:258], ONESF[:, :], ACC[ch % 2][:, 0:258], ch == 0, ch == 7,
               tp=(0, 96), r=[f"acc{ch % 2}", "ONESF"], w=["pSP2"])
        OP("ACT", lambda e: e.activation(RS(0)[:, 0:258], STAT[64:65, 0:258],
           AF.Copy, bias=0.0, scale=1.0 / 1024.0), r=["pSP2"], w=["row0"])
        OP("ACT", lambda e: e.activation(RS(4)[:, 0:258], STAT[96:97, 0:258],
           AF.Copy, bias=0.0, scale=1.0 / 1024.0), r=["pSP2"], w=["row4"])
        OP("DVE", lambda e: e.tensor_tensor(out=RS(8)[:, 0:258], in0=RS(0)[:, 0:258],
           in1=RS(0)[:, 0:258], op=ALU.mult), r=["row0"], w=["row8"])
        OP("DVE", lambda e: e.tensor_tensor(out=RS(4)[:, 0:258], in0=RS(4)[:, 0:258],
           in1=RS(8)[:, 0:258], op=ALU.subtract), r=["row4", "row8"], w=["row4"])
        OP("ACT", lambda e: e.activation(RS(8)[:, 0:258], RS(4)[:, 0:258],
           AF.Ln, bias=CONSTS[0:1, 0:1], scale=1.0), r=["row4", "CONSTS"], w=["row8"])
        OP("ACT", lambda e: e.activation(RS(8)[:, 0:258], RS(8)[:, 0:258],
           AF.Exp, bias=0.0, scale=-0.5), r=["row8"], w=["row8"])
        bcast_build(RS(0)[:, 0:258], BC[0], 258, ["row0"], "bc0", tag=f"l2m{t}")
        bcast_build(RS(8)[:, 0:258], BC[1], 258, ["row8"], "bc1", tag=f"l2r{t}")
        for ch in range(8):
            xs = XT[t][:, 260 * ch + 1: 260 * ch + 259]
            OP("DVE", lambda e, x_=xs: e.tensor_tensor(out=ACC[2][:, 0:258],
               in0=x_, in1=BC[0][:, 0:258], op=ALU.subtract),
               r=[xk, "bc0"], w=["acc2"])
            OP("DVE", lambda e, ch_=ch: e.tensor_tensor(
               out=XQ[:, 258 * ch_:258 * (ch_ + 1)], in0=ACC[2][:, 0:258],
               in1=BC[1][:, 0:258], op=ALU.mult), r=["acc2", "bc1"], w=["xq"],
               tag=f"xh{t}_{ch}")
        for g in range(4):
            for k8 in range(8):
                MM(FA[:, 0:256], W2S[0][:, 512 * k8 + 128 * g: 512 * k8 + 128 * (g + 1)],
                   XQ[:, 258 * k8 + 1: 258 * k8 + 257], k8 == 0, k8 == 7,
                   r=["w2s0", "xq"], w=["pFA", "pSC"], tag=f"q{t}_{g}_{k8}")
            OP("ACT", lambda e, t_=t, g_=g: e.copy(QT[t_][:, 256 * g_:256 * (g_ + 1)],
               FA[:, 0:256]), r=["pFA"], w=[f"q{t}"], tag=f"qe{t}_{g}")
        for k8 in range(8):
            MM(FB[:, 0:256], W2S[1][:, 128 * k8:128 * (k8 + 1)],
               XQ[:, 258 * k8 + 1: 258 * k8 + 257], k8 == 0, k8 == 7,
               r=["w2s1", "xq"], w=["pFB", "pSC"], tag=f"k{t}_{k8}")
        OP("ACT", lambda e, t_=t: e.copy(KL[t_][:, :], FB[:, 0:256]),
           r=["pFB"], w=[f"kl{t}"], tag=f"ke{t}")
        for tb in range(2):
            for k8 in range(8):
                MM(FA[:, 0:256], XQ[:, 258 * k8 + 1 + 128 * tb: 258 * k8 + 1 + 128 * (tb + 1)],
                   W2S[1][:, 1024 + 256 * k8: 1024 + 256 * (k8 + 1)], k8 == 0, k8 == 7,
                   r=["w2s1", "xq"], w=["pFA", "pSC"], tag=f"v{t}_{tb}_{k8}")
            OP("ACT", lambda e, t_=t, tb_=tb: e.copy(VL[t_][:, 256 * tb_:256 * (tb_ + 1)],
               FA[:, 0:256]), r=["pFA"], w=[f"vl{t}"], tag=f"ve{t}_{tb}")
        DMA("GP", kcond[:, 256 * t:256 * (t + 1)], KL[t][:, :], r=[f"kl{t}"],
            tag=f"kc{t}")
        DMA("GP", vcond[256 * t:256 * t + 128, :], VL[t][:, 0:256], r=[f"vl{t}"],
            tag=f"vc{t}a")
        DMA("GP", vcond[256 * t + 128:256 * (t + 1), :], VL[t][:, 256:512], r=[f"vl{t}"],
            tag=f"vc{t}b")

    DMA("GP", dbg5[0, :, :], QT[4][:, :], r=["q4"], tag="dbg5q")
    DMA("GP", dbg5[1, :, 0:256], KL[4][:, :], r=["kl4"], tag="dbg5k")
    DMA("GP", dbg5[2, :, 0:512], VL[4][:, :], r=["vl4"], tag="dbg5v")
    DMA("GP", dbg5[3, :, 0:258], XQ[:, 0:258], r=["xq"], tag="dbg5x")
    if PHASES >= 3:
      OP("GP", lambda e: e.collective_compute(
        "AllGather", ALU.bypass, replica_groups=[list(range(8))],
        ins=[kcond.ap().opt()], outs=[kgatd.ap().opt()]),
        r=[f"kc{t}" for t in range(8)],
        w=["kgat"], cc="ccKV", tag="agk")
      OP("GP", lambda e: e.collective_compute(
        "AllGather", ALU.bypass, replica_groups=[list(range(8))],
        ins=[vcond.ap().opt()], outs=[vgatd.ap().opt()]),
        r=[f"vc{t}a" for t in range(8)] + [f"vc{t}b" for t in range(8)],
        w=["vgat"], cc="ccKV2", tag="agv")

    ei = 0

    def cring_refill(i0):
        n = min(8, NENT - i0)
        for m in range(4):
            DMA("GP", CRING[32 * m:32 * m + 1, 0:256 * n],
                ctabd[m:m + 1, 256 * i0:256 * (i0 + n)],
                w=["cring"], tag=f"cr{i0}_{m}")

    def attn_iter(ent, epar, kh_handle, kcol, kv_v, kvkeys, first_ctx, last_ctx,
                  diag_sb=None):
        nonlocal ei
        k, g, hp = ent["k"], ent["g"], ent["hp"]
        t = k
        ring = ei % 8
        if ring == 0:
            cring_refill(ei)
        glocal = 0 if g in (3, 1) else 1
        CTX = CT0 if glocal == 0 else CT1
        ctxk = "pCT0" if glocal == 0 else "pCT1"
        wdt = 256
        for m in range(4 if 's' in APARTS else 0):
            MM(SC[:, 512 * m:512 * m + wdt],
               kh_handle[32 * m:32 * (m + 1), kcol:kcol + 128],
               QT[t][32 * m:32 * (m + 1), 256 * g: 256 * g + wdt],
               True, False, tp=(32 * m, 0),
               r=kvkeys + [f"q{t}"], w=["pSC"], tag=f"sc{ei}_{m}")
        for m in range(4 if 'c' in APARTS else 0):
            MM(SC[:, 512 * m:512 * m + wdt],
               CRING[32 * m:32 * m + 1, 256 * ring:256 * ring + 128],
               ONESB[32 * m:32 * m + 1, 0:wdt],
               False, False, tp=(32 * m, 0),
               r=["cring", "ONESB"], w=["pSC"], tag=f"cj{ei}_{m}")
        for m in range(4 if 'c' in APARTS else 0):
            MM(SC[:, 512 * m:512 * m + wdt],
               CRING[32 * m:32 * m + 1, 256 * ring + 128:256 * ring + 256],
               ONESB[32 * m:32 * m + 1, 0:wdt],
               False, True, tp=(32 * m, 0),
               r=["cring", "ONESB"], w=["pSC"], tag=f"cl{ei}_{m}")
        if 'e' not in APARTS:
            pass
        elif diag_sb == 1:
            OP("GP", lambda e, p=epar: e.memset(
               ET[p][:, :].rearrange("p (s f) -> p s f", s=4)[:, :, 0:128], 0.0),
               w=[f"e{epar}"], tag=f"ez{ei}")
            OP("ACT", lambda e, p=epar: e.activation(
               ET[p][:, :].rearrange("p (s f) -> p s f", s=4)[:, :, 128:256],
               SC[:, :].rearrange("p (s f) -> p s f", s=4)[:, :, 128:256],
               AF.Exp, bias=0.0, scale=1.0 / SQ32),
               r=["pSC"], w=[f"e{epar}"], tag=f"ex{ei}")
        else:
            OP("ACT", lambda e, p=epar: e.activation(
               ET[p][:, :].rearrange("p (s f) -> p s f", s=4)[:, :, 0:256],
               SC[:, :].rearrange("p (s f) -> p s f", s=4)[:, :, 0:256],
               AF.Exp, bias=0.0, scale=1.0 / SQ32),
               r=["pSC"], w=[f"e{epar}"], tag=f"ex{ei}")
        if diag_sb is not None and 'd' in APARTS:
            coff = 0 if diag_sb == 0 else 128
            OP("DVE", lambda e, p=epar, co=coff: e.tensor_tensor(
               out=ET[p][:, :].rearrange("p (s f) -> p s f", s=4)[:, :, co:co + 128],
               in0=ET[p][:, :].rearrange("p (s f) -> p s f", s=4)[:, :, co:co + 128],
               in1=TRI[:, None, :].broadcast_to([128, 4, 128]),
               op=ALU.mult), r=[f"e{epar}", "TRI"], w=[f"e{epar}"], tag=f"tm{ei}")
        for pr in range(2 if 'x' in APARTS else 0):
            for half in range(2):
                hm = 2 * pr + half
                MM(CTX[64 * half:64 * (half + 1), 256 * pr:256 * (pr + 1)],
                   kv_v[:, 64 * hm:64 * (hm + 1)],
                   ET[epar][:, 256 * hm:256 * hm + 256],
                   first_ctx, last_ctx, tp=(0, 64 * half),
                   r=kvkeys + [f"e{epar}"], w=[ctxk], tag=f"cx{ei}_{pr}_{half}")
        if ent["kind"] == "diag" and k == 4 and hp == 1 and g == 1 and ent.get("sb") == 0:
            OP("ACT", lambda e: e.copy(SL[:, 5696:5952], SC[:, 0:256]),
               r=["pSC"], w=["sl"], tag="dbg7a")
            OP("ACT", lambda e: e.copy(SL[:, 5952:6208], SC[:, 512:768]),
               r=["pSC"], w=["sl"], tag="dbg7b")
            DMA("GP", dbg4[0, :, 0:256], SL[:, 5696:5952], r=["sl"], tag="dbg7ad")
            DMA("GP", dbg4[1, :, 0:256], SL[:, 5952:6208], r=["sl"], tag="dbg7bd")
            OP("DVE", lambda e, p=epar: e.tensor_copy(SL[:, 6208:6720], ET[p][:, 0:512]),
               r=[f"e{epar}"], w=["sl"], tag="dbg7c")
            DMA("GP", dbg4[2, :, :], SL[:, 6208:6720], r=["sl"], tag="dbg7cd")
        if ent["kind"] == "diag" and k == 4 and hp == 1 and g == 1 and ent.get("sb") == 1:
            OP("DVE", lambda e, p=epar: e.tensor_copy(SL[:, 6720:7232], ET[p][:, 0:512]),
               r=[f"e{epar}"], w=["sl"], tag="dbg7e")
            DMA("GP", dbg4[1, :, 256:512], SL[:, 6720:6976], r=["sl"], tag="dbg7ed")
        for sh in range(2 if 'm' in APARTS else 0):
            base = 32 * (2 * glocal + sh)
            MM(SMS[base:base + 1, 0:512], ONESB[:, 0:1],
               ET[epar][:, 512 * sh:512 * (sh + 1)],
               first_ctx, last_ctx, tp=(0, base),
               r=[f"e{epar}", "ONESB"], w=["pSMS"], tag=f"sm{ei}_{sh}")
        ei += 1

    idx = 0
    hp_count = 0
    while idx < (NENT if PHASES >= 4 else 0):
        ent = ENTRIES[idx]
        k, hp = ent["k"], ent["hp"]
        t = k
        b_k = k // 4
        grp = []
        while idx < NENT and ENTRIES[idx]["k"] == k and ENTRIES[idx]["hp"] == hp:
            grp.append(ENTRIES[idx])
            idx += 1
        hp_count += 1
        if hp_count > NHP:
            continue
        packs = (3, 2) if hp == 0 else (1, 0)
        ng = {g: sum(1 for e2 in grp if e2["g"] == g) for g in packs}
        seen = {g: 0 for g in packs}
        kvpar = 0
        lastJ = None
        for ent2 in grp:
            g = ent2["g"]
            fc = seen[g] == 0
            lc = seen[g] == ng[g] - 1
            seen[g] += 1
            if ent2["kind"] == "far":
                J = ent2["J"]
                if J != lastJ:
                    kvpar ^= 1
                    pp2 = J // 2
                    r_ = owner(b_k, pp2)
                    ks_ = 4 * b_k + pp2 // 8
                    hh = J % 2
                    DMA("GP", KB[kvpar][:, :],
                        kgatd[128 * r_:128 * (r_ + 1),
                              256 * ks_ + 128 * hh:256 * ks_ + 128 * (hh + 1)],
                        r=["kgat"], w=[f"kb{kvpar}"], tag=f"kbl{k}_{hp}_{J}")
                    DMA("GP", VB[kvpar][:, :],
                        vgatd[2048 * r_ + 256 * ks_ + 128 * hh:
                              2048 * r_ + 256 * ks_ + 128 * (hh + 1), :],
                        r=["vgat"], w=[f"vb{kvpar}"], tag=f"vbl{k}_{hp}_{J}")
                    lastJ = J
                attn_iter(ent2, ei % 2, KB[kvpar], 0, VB[kvpar][:, :],
                          [f"kb{kvpar}", f"vb{kvpar}"], fc, lc)
            else:
                sb_ = ent2["sb"]
                attn_iter(ent2, ei % 2, KL[t], 128 * sb_,
                          VL[t][:, 256 * sb_:256 * (sb_ + 1)],
                          [f"kl{t}", f"vl{t}"], fc, lc, diag_sb=sb_)
        if 'u' not in APARTS:
            continue
        if k == 4:
            di = 0 if hp == 0 else 3
            OP("ACT", lambda e, d_=di: e.copy(SL[:, 4160:4672], CT0[:, :]),
               r=["pCT0"], w=["sl"], tag=f"dbg4a{hp}")
            DMA("GP", dbg4[di, :, :], SL[:, 4160:4672], r=["sl"], tag=f"dbg4ad{hp}")
            OP("ACT", lambda e, d_=di: e.copy(SL[:, 4672:5184], CT1[:, :]),
               r=["pCT1"], w=["sl"], tag=f"dbg4b{hp}")
            DMA("GP", dbg4[di + 1, :, :], SL[:, 4672:5184], r=["sl"], tag=f"dbg4bd{hp}")
            for ri_ in range(4):
                OP("ACT", lambda e, r2=ri_: e.copy(SL[32 * r2:32 * r2 + 1, 5184:5696],
                   SMS[32 * r2:32 * r2 + 1, :]), r=["pSMS"], w=["sl"], tag=f"dbg4c{hp}_{ri_}")
            DMA("GP", dbg4[di + 2, :, :], SL[:, 5184:5696], r=["sl"], tag=f"dbg4cd{hp}")
        for sh in range(4):
            OP("DVE", lambda e, b=32 * sh: e.tensor_copy(RECIP[b:b + 1, 0:512],
               SMS[b:b + 1, 0:512]), r=["pSMS"], w=["recip"], tag=f"se{k}_{hp}_{sh}")
        OP("DVE", lambda e: e.tensor_scalar(out=RECIP[:, :], in0=RECIP[:, :],
           scalar1=1e-30, scalar2=None, op0=ALU.add), r=["recip"], w=["recip"])
        OP("ACT", lambda e: e.activation(RECIP[:, :], RECIP[:, :], AF.Ln),
           r=["recip"], w=["recip"])
        OP("ACT", lambda e: e.activation(RECIP[:, :], RECIP[:, :], AF.Exp,
           bias=0.0, scale=-1.0), r=["recip"], w=["recip"])
        for glocal, g in enumerate(packs):
            CTX = CT0 if glocal == 0 else CT1
            ctxk = "pCT0" if glocal == 0 else "pCT1"
            for pr in range(2):
                rb = 32 * (2 * glocal + pr)
                rpar = pr % 2
                MM(SP2[:, 0:256], SELR[rb:rb + 1, 0:128], RECIP[rb:rb + 1, 0:256],
                   True, False, tp=(rb, 0), r=["recip", "SELR"], w=["pSP2"],
                   tag=f"r1{k}{hp}{g}{pr}")
                MM(SP2[:, 0:256], SELR[rb:rb + 1, 128:256], RECIP[rb:rb + 1, 256:512],
                   False, True, tp=(rb, 0), r=["recip", "SELR"], w=["pSP2"],
                   tag=f"r2{k}{hp}{g}{pr}")
                OP("ACT", lambda e, rp=rpar: e.copy(RSB[rp][:, :], SP2[:, 0:256]),
                   r=["pSP2"], w=[f"rsb{rpar}"], tag=f"re{k}{hp}{g}{pr}")
                d2 = 2 * g + pr
                OP("DVE", lambda e, rp=rpar, P=CTX, pr_=pr, d_=d2: e.tensor_tensor(
                   out=CTXN[:, 256 * d_:256 * (d_ + 1)],
                   in0=P[:, 256 * pr_:256 * (pr_ + 1)], in1=RSB[rp][:, :],
                   op=ALU.mult), r=[f"rsb{rpar}", ctxk], w=["ctxn"],
                   tag=f"cn{k}{hp}{g}{pr}")
        if hp == 1 and k == 4:
            DMA("GP", dbg4[6, :, :], CTXN[:, 512:1024], r=["ctxn"], tag="dbg4e")
            DMA("GP", dbg4[7, :, :], CTXN[:, 0:512], r=["ctxn"], tag="dbg4f")
        if hp == 1:
            for ec in range(8):
                par = ec % 2
                DMA("GP", WPS[par][:, :].rearrange("p (k c) -> p k c", c=128),
                    wpd[:, 128 * ec:128 * (ec + 1)].rearrange("(k p) c -> p k c", p=128),
                    w=[f"wps{par}"], tag=f"wpl{k}_{ec}")
                for d2 in range(8):
                    MM(SP2[:, 0:256], WPS[par][:, 128 * d2:128 * (d2 + 1)],
                       CTXN[:, 256 * d2:256 * (d2 + 1)], d2 == 0, d2 == 7,
                       r=[f"wps{par}", "ctxn"], w=["pSP2"], tag=f"pj{k}_{ec}_{d2}")
                xs = XT[t][:, 260 * ec + 2: 260 * ec + 258]
                OP("DVE", lambda e, x_=xs, ec_=ec: e.scalar_tensor_tensor(
                   out=x_, in0=SP2[:, 0:256], scalar=BP[:, ec_:ec_ + 1], in1=x_,
                   op0=ALU.add, op1=ALU.add), r=["pSP2", "BP", f"X{t}"],
                   w=[f"X{t}"], tag=f"pe{k}_{ec}")

    for ch in range(8 if (PHASES >= 4 and NHP >= 16) else 0):
        for t in range(8):
            OP("DVE", lambda e, t_=t, ch_=ch: e.tensor_copy(
               HSTG[:, 2 * t_:2 * t_ + 1], XT[t_][:, 260 * ch_ + 2:260 * ch_ + 3]),
               r=[f"X{t}"], w=["hstg"], tag=f"hs{ch}_{t}a")
            OP("DVE", lambda e, t_=t, ch_=ch: e.tensor_copy(
               HSTG[:, 2 * t_ + 1:2 * t_ + 2], XT[t_][:, 260 * ch_ + 257:260 * ch_ + 258]),
               r=[f"X{t}"], w=["hstg"], tag=f"hs{ch}_{t}b")
        OP("PE", lambda e: e.transpose(SP2[0:16, 0:128], HSTG[:, 0:16], IDF[:, :]),
           r=["hstg", "IDF"], w=["pSP2"], tag=f"ht{ch}")
        OP("ACT", lambda e, ch_=ch: e.copy(SL[0:16, 128 * ch_:128 * (ch_ + 1)],
           SP2[0:16, 0:128]), r=["pSP2"], w=["halt", "sl"], tag=f"he{ch}")
    if PHASES >= 4 and NHP >= 16:
      DMA("GP", halind[:, :], SL[0:16, 0:1024], r=["halt", "sl"], tag="hst")
      OP("GP", lambda e: e.collective_compute(
        "AllGather", ALU.bypass, replica_groups=[list(range(8))],
        ins=[halind.ap().opt()], outs=[halgd.ap().opt()]),
        r=["hst", "halt"], w=["halg"], cc="ccHAL", tag="aghal")
    if PHASES >= 4:
        OP("GP", lambda e: e.memset(SL[0:48, 1024:2048], 0.0),
           w=["cands", "sl"], tag="czero")
    for k in range(8 if PHASES >= 4 else 0):
        t = k
        nlo = max(0, 2 * k - 2)
        nhi = min(16, 2 * k + 4)
        nq = nhi - nlo
        for r_ in range(8):
            DMA("GP", SL[6 * r_:6 * r_ + nq, 1024:2048],
                halgd[16 * r_ + nlo:16 * r_ + nhi, :],
                r=["halg"], w=["cands", "sl"], tag=f"cl{k}_{r_}")
        for ch in range(8):
            MM(SP2[0:128, 0:2], SL[0:48, 1024 + 128 * ch:1024 + 128 * (ch + 1)],
               SELC[:, 2 * k:2 * (k + 1)], True, True,
               r=["cands", "sl"], w=["pSP2"], tag=f"hm{k}_{ch}")
            OP("ACT", lambda e, t_=t, ch_=ch: e.copy(
               XT[t_][:, 260 * ch_ + 1:260 * ch_ + 2], SP2[0:128, 0:1]),
               r=["pSP2"], w=[f"X{t}"], tag=f"hwl{k}_{ch}")
            OP("ACT", lambda e, t_=t, ch_=ch: e.copy(
               XT[t_][:, 260 * ch_ + 258:260 * ch_ + 259], SP2[0:128, 1:2]),
               r=["pSP2"], w=[f"X{t}"], tag=f"hwr{k}_{ch}")

    for t in range(8 if (PHASES >= 4 and NHP >= 16) else 0):
        for ch in range(8):
            DMA("GP", dbg2[t, 128 * ch:128 * (ch + 1), :],
                XT[t][:, 260 * ch:260 * (ch + 1)], r=[f"X{t}"], tag=f"dbg2_{t}_{ch}")
    for t in (range(7, -1, -1) if PHASES >= 5 else []):
        ffn(t, 258, 3, 0)
        ffn_mm2(t, 258, 3)
        for ch in range(8):
            DMA("GP", outd[t, 128 * ch:128 * (ch + 1), :],
                XT[t][:, 260 * ch + 2:260 * ch + 258],
                r=[f"X{t}"], tag=f"out{t}_{ch}")

    def final_wait(e):
        return e.engine_nop()
    OP("GP", final_wait, r=[f"X{t}" for t in range(8)], tag="fin")

    emis = sch.finalize()
    return nc, st, semh, emis, sch


def _emit(nc, st, semh, emis, sch):
    tot = defaultdict(int)
    for o in sch.ops:
        if "ms" in o and o["ms"][1] is not None:
            s, v, _ = o["ms"]
            tot[s] = max(tot[s], v)
    with nc.Block() as block:
        def mk(engname):
            def run(e):
                for ws, fn, ms, tag in emis[engname]:
                    for s, v in ws:
                        e.wait_ge(semh[s], v)
                    ins = fn(e)
                    if ms is not None:
                        s, v, step = ms
                        if v is None:
                            ins.then_inc(semh[s])
                        else:
                            ins.then_inc(semh[s], step)
                if engname == "GP":
                    for s in ("dGP", "dSYNC", "sPE", "sACT", "sDVE"):
                        if tot.get(s):
                            e.wait_ge(semh[s], tot[s])
            return run
        block.tensor(mk("PE"))
        block.scalar(mk("ACT"))
        block.vector(mk("DVE"))
        block.gpsimd(mk("GP"))
        block.sync(mk("SYNC"))
    st.close()
    return nc


def host_inputs(c, inputs):
    x = np.asarray(inputs["L0_attn_x"])
    xs = np.zeros((8, D, 260), dtype=np.float32)
    sel = np.zeros((48, 16), dtype=np.float32)
    edge = np.ones((2, 128, 258), dtype=np.float32)
    for k in range(8):
        b, pp = core_tile(c, k)
        S = TS * pp
        lo, hi = S - 2, S + 258
        slo, shi = max(lo, 0), min(hi, T)
        if shi > slo:
            xs[k][:, slo - lo: shi - lo] = np.ascontiguousarray(x[b, slo:shi, :].T)
        nlo = max(0, 2 * k - 2)
        if pp - 1 >= 0:
            r1 = owner(b, pp - 1)
            k1 = 4 * b + (pp - 1) // 8
            sel[6 * r1 + (2 * k1 + 1 - nlo), 2 * k] = 1.0
        if pp + 1 < NT:
            r2 = owner(b, pp + 1)
            k2 = 4 * b + (pp + 1) // 8
            sel[6 * r2 + (2 * k2 + 0 - nlo), 2 * k + 1] = 1.0
        ei_, eb_ = k // 4, 32 * (k % 4)
        if pp == 0:
            edge[ei_, eb_, 0] = 0.0
        if pp == NT - 1:
            edge[ei_, eb_, 257] = 0.0
    ctab = np.zeros((4, NENT * 256), dtype=bfloat16)
    p128 = np.arange(128, dtype=np.float64)
    for i, e in enumerate(ENTRIES):
        k, g = e["k"], e["g"]
        b, pp = core_tile(c, k)
        S = TS * pp
        A = S + 128.0
        for m in range(4):
            h = 4 * g + m
            sl_ = SLOPES[h]
            if e["kind"] == "far":
                J = e["J"]
                if J >= 2 * pp:
                    vals = np.full(128, -1e30)
                else:
                    vals = sl_ * (128.0 * J + p128 - A)
            else:
                vals = sl_ * (128.0 * e["sb"] + p128 - 128.0)
            hi = vals.astype(np.float32).astype(bfloat16)
            lo = (vals.astype(np.float32)
                  - hi.astype(np.float32)).astype(bfloat16)
            ctab[m, 256 * i:256 * i + 128] = hi
            ctab[m, 256 * i + 128:256 * (i + 1)] = lo
    return dict(x=xs, sel=sel, edge=edge, ctab=ctab)


def kernel(L0_attn_x, L0_attn_Wq, L0_attn_Wk, L0_attn_Wv, L0_attn_Wp, L0_attn_bp,
           L0_ffn_w1, L0_ffn_conv_w, L0_ffn_conv_b, L0_ffn_w2):
    x = np.ascontiguousarray(np.asarray(L0_attn_x, dtype=np.float32))
    w1T = np.ascontiguousarray(np.asarray(L0_ffn_w1, dtype=np.float32).T)
    w2T = np.ascontiguousarray(np.asarray(L0_ffn_w2, dtype=np.float32).T)
    cw = np.asarray(L0_ffn_conv_w, dtype=np.float32).reshape(C, 3)
    cb = np.asarray(L0_ffn_conv_b, dtype=np.float32)
    convw = np.zeros((128, 128), dtype=np.float32)
    for kk in range(3):
        convw[:, 32 * kk:32 * (kk + 1)] = cw[:, kk].reshape(32, 128).T
    convw[:, 96:128] = cb.reshape(32, 128).T
    bp = np.ascontiguousarray(np.asarray(L0_attn_bp, dtype=np.float32).reshape(8, 128).T)
    tri = np.triu(np.ones((128, 128), dtype=np.float32), 0).astype(bfloat16)
    identf = np.eye(128, dtype=np.float32)
    shared = dict(
        w1T=w1T, w2T=w2T, convw=convw, bp=bp,
        Wq=np.ascontiguousarray(np.asarray(L0_attn_Wq, dtype=np.float32)),
        Wk=np.ascontiguousarray(np.asarray(L0_attn_Wk, dtype=np.float32)),
        Wv=np.ascontiguousarray(np.asarray(L0_attn_Wv, dtype=np.float32)),
        Wp=np.ascontiguousarray(np.asarray(L0_attn_Wp, dtype=np.float32)),
        tri=tri, identf=identf,
    )
    sr = np.zeros((128, 256), np.float32)
    for rb_ in (0, 32, 64, 96):
        sr[rb_, 0:64] = 1.0
        sr[rb_, 192:256] = 1.0
    shared["selrow"] = sr

    nc, st, semh, emis, sch = build_program()
    nc = _emit(nc, st, semh, emis, sch)

    in_maps = []
    for c in range(8):
        m = dict(shared)
        m.update(host_inputs(c, dict(L0_attn_x=x)))
        in_maps.append(m)
    res = run_bass_kernel_spmd(nc, in_maps, core_ids=list(range(8)))

    out = np.empty((NB, T, D), dtype=np.float32)
    for c in range(8):
        xo = res.results[c]["xout"]
        for k in range(8):
            b, pp = core_tile(c, k)
            out[b, TS * pp:TS * (pp + 1), :] = xo[k].T
    return out


# version 11 (final)
# speedup vs baseline: 1.0002x; 1.0002x over previous
import numpy as np
import ml_dtypes
from collections import defaultdict
from contextlib import ExitStack

import concourse.bass as bass
import concourse.mybir as mybir
from concourse.bass_utils import run_bass_kernel_spmd

F32 = mybir.dt.float32
BF16 = mybir.dt.bfloat16
I8 = mybir.dt.int8
I32 = mybir.dt.int32
AF = mybir.ActivationFunctionType
ALU = mybir.AluOpType
AX = mybir.AxisListType
bfloat16 = ml_dtypes.bfloat16

D = 1024
C = 4096
T = 8192
NB = 2
H = 16
NT = 32
TS = 256
SQ32 = float(np.sqrt(32.0))
LOG127 = float(np.log(127.0))
CUT = 50.0
EPS = 1e-5
PHASES = 5
NHP = 99
DBG = False
APARTS = set('sciexmude')

SLOPES = [2.0 ** (-8.0 * (h + 1) / H) for h in range(H)]
DPACK = [max(CUT * SQ32 / SLOPES[4 * g + m] for m in range(4)) for g in range(4)]

def core_tile(c, k):
    b = k // 4
    pp = 8 * (k % 4) + ((c - 4 * b) % 8)
    return b, pp

def owner(b, pp):
    return (pp + 4 * b) % 8

EK = [0] * 8
FKG = [[0] * 4 for _ in range(8)]
for k in range(8):
    q = k % 4
    EK[k] = 16 * q + 14
    S0 = 2048 * q
    for g in range(4):
        jmin = max(0, int(np.ceil((S0 - 128 - DPACK[g]) / 128.0)))
        FKG[k][g] = min(jmin, EK[k])

def schedule_entries():
    ents = []
    for k in range(7, -1, -1):
        for hp in range(2):
            packs = (3, 2) if hp == 0 else (1, 0)
            f0 = min(FKG[k][g] for g in packs)
            for J in range(f0, EK[k]):
                for g in packs:
                    if FKG[k][g] <= J:
                        ents.append(dict(kind="far", k=k, hp=hp, J=J, g=g))
            for g in packs:
                for sb_ in range(2):
                    ents.append(dict(kind="diag", k=k, hp=hp, g=g, sb=sb_))
    return ents

ENTRIES = schedule_entries()
NENT = len(ENTRIES)

ENGS = ("PE", "ACT", "DVE", "GP", "SYNC")


class Sched:
    def __init__(self):
        self.ops = []
        self.last_w = {}
        self.readers = {}

    def op(self, eng, fn, r=(), w=(), dma=False, cc=None, tag="", fence=False):
        i = len(self.ops)
        deps = set()
        for kk in r:
            if kk in self.last_w:
                deps.add(self.last_w[kk])
        for kk in w:
            if kk in self.last_w:
                deps.add(self.last_w[kk])
            for rd in self.readers.get(kk, ()):
                deps.add(rd)
        self.ops.append(dict(i=i, eng=eng, fn=fn, deps=deps, dma=dma, cc=cc,
                             need=False, tag=tag, fence=fence))
        for kk in r:
            self.readers.setdefault(kk, []).append(i)
        for kk in w:
            self.last_w[kk] = i
            self.readers[kk] = []
        return i

    def finalize(self):
        for o in self.ops:
            real = []
            for d in sorted(o["deps"]):
                dop = self.ops[d]
                if (dop["eng"] == o["eng"] and not dop["dma"]
                        and dop["cc"] is None and not dop["fence"]):
                    continue
                real.append(d)
                dop["need"] = True
            o["wdeps"] = real
        cnt = defaultdict(int)
        for o in self.ops:
            if o["cc"] is not None:
                o["ms"] = (o["cc"], None, None)
                continue
            if o["dma"]:
                o["need"] = True
            if o["need"]:
                s = ("d" if o["dma"] else "s") + o["eng"]
                step = 16 if o["dma"] else 1
                cnt[s] += step
                o["ms"] = (s, cnt[s], step)
        emis = {e: [] for e in ENGS}
        wm = {e: defaultdict(int) for e in ENGS}
        for o in self.ops:
            ws = []
            for d in o["wdeps"]:
                dop = self.ops[d]
                s, v, _ = dop["ms"]
                if v is None:
                    v = 1
                if wm[o["eng"]][s] < v:
                    ws.append((s, v))
                    wm[o["eng"]][s] = v
            emis[o["eng"]].append((ws, o["fn"], o.get("ms"), o["tag"]))
        return emis


def build_program():
    nc = bass.Bass("TRN2", target_bir_lowering=False, detect_race_conditions=False)
    sch = Sched()
    st = ExitStack()

    xd = nc.dram_tensor("x", [8, D, 260], F32, kind="ExternalInput")
    w1Td = nc.dram_tensor("w1T", [D, C], F32, kind="ExternalInput")
    w2Td = nc.dram_tensor("w2T", [C, D], F32, kind="ExternalInput")
    convd = nc.dram_tensor("convw", [128, 128], F32, kind="ExternalInput")
    wqd = nc.dram_tensor("Wq", [D, 512], F32, kind="ExternalInput")
    wkd = nc.dram_tensor("Wk", [D, 128], F32, kind="ExternalInput")
    wvd = nc.dram_tensor("Wv", [D, 256], F32, kind="ExternalInput")
    wpd = nc.dram_tensor("Wp", [D, D], F32, kind="ExternalInput")
    bpd = nc.dram_tensor("bp", [128, 8], F32, kind="ExternalInput")
    ctabd = nc.dram_tensor("ctab", [4, NENT * 256], BF16, kind="ExternalInput")
    seld = nc.dram_tensor("sel", [48, 16], F32, kind="ExternalInput")
    edged = nc.dram_tensor("edge", [2, 128, 258], F32, kind="ExternalInput")
    trid = nc.dram_tensor("tri", [128, 128], BF16, kind="ExternalInput")
    selrowd = nc.dram_tensor("selrow", [128, 256], F32, kind="ExternalInput")
    idfd = nc.dram_tensor("identf", [128, 128], F32, kind="ExternalInput")

    outd = nc.dram_tensor("xout", [8, D, 256], F32, kind="ExternalOutput")
    dbg1 = nc.dram_tensor("dbg1", [8, D, 260], F32, kind="ExternalOutput")
    dbg2 = nc.dram_tensor("dbg2", [8, D, 260], F32, kind="ExternalOutput")
    dbg3 = nc.dram_tensor("dbg3", [12, 128, 260], F32, kind="ExternalOutput")
    dbg4 = nc.dram_tensor("dbg4", [8, 128, 512], F32, kind="ExternalOutput")
    dbg5 = nc.dram_tensor("dbg5", [4, 128, 1024], F32, kind="ExternalOutput")

    w1qd = nc.dram_tensor("w1q", [D, C], BF16)
    w2qd = nc.dram_tensor("w2q", [C, D], BF16)
    kcond = nc.dram_tensor("kcon", [128, 2048], BF16)
    vcond = nc.dram_tensor("vcon", [2048, 256], BF16)
    kgatd = nc.dram_tensor("kgat", [1024, 2048], BF16)
    vgatd = nc.dram_tensor("vgat", [16384, 256], BF16)
    halind = nc.dram_tensor("halin", [16, 1024], F32)
    halgd = nc.dram_tensor("halg", [128, 1024], F32)

    def sb(name, shape, dt):
        return st.enter_context(nc.sbuf_tensor(name, shape, dt))

    def ps(name, shape):
        return st.enter_context(nc.psum_tensor(name, shape, F32))

    XT = [sb(f"XT{t}", [128, 8 * 260], F32) for t in range(8)]
    SL = sb("SL", [128, 8256], F32)
    KH = [sb("KH0", [128, 32 * 258], BF16)]
    XQ = sb("XQ", [128, 8 * 260], BF16)
    W1S = [sb(f"W1S{i}", [128, 1024], BF16) for i in range(2)]
    W2S = [sb(f"W2S{i}", [128, 4096], BF16) for i in range(2)]
    ACC = [sb(f"ACC{i}", [128, 260], F32) for i in range(3)]
    H1S = [sb(f"H1S{i}", [128, 260], F32) for i in range(2)]
    RWS = [sb(f"RWS{i}", [128, 260], F32) for i in range(3)]
    BC = [sb(f"BC{i}", [128, 260], F32) for i in range(3)]
    CW = sb("CW", [128, 128], F32)
    QT = [sb(f"QT{t}", [128, 1024], BF16) for t in range(8)]
    KL = [sb(f"KL{t}", [128, 256], BF16) for t in range(8)]
    VL = [sb(f"VL{t}", [128, 512], BF16) for t in range(8)]
    KB = [sb(f"KB{i}", [128, 128], BF16) for i in range(2)]
    VB = [sb(f"VB{i}", [128, 256], BF16) for i in range(2)]
    ET = [sb(f"ET{i}", [128, 1024], BF16) for i in range(2)]
    CRING = sb("CRING", [128, 2048], BF16)
    CTXN = sb("CTXN", [128, 2048], BF16)
    RECIP = sb("RECIP", [128, 512], F32)
    RSB = [sb(f"RSB{i}", [128, 256], F32) for i in range(2)]
    WPS = [sb(f"WPS{i}", [128, 1024], BF16) for i in range(2)]
    TRI = sb("TRI", [128, 128], BF16)
    SELR = sb("SELR", [128, 256], F32)
    IDF = sb("IDF", [128, 128], F32)
    ONESB = sb("ONESB", [128, 256], BF16)
    ONESF = sb("ONESF", [128, 1], F32)
    CONSTS = sb("CONSTS", [128, 4], F32)
    ONE1F = sb("ONE1F", [1, 128], F32)
    EDG = [sb(f"EDG{i}", [128, 258], F32) for i in range(2)]
    BP = sb("BP", [128, 8], F32)
    MH = sb("MH", [128, 260], F32)
    TCOL = sb("TCOL", [128, 4], F32)
    SELC = sb("SELC", [48, 16], F32)
    HSTG = sb("HSTG", [128, 16], F32)
    HALT = SL[0:16, 0:1024]
    CANDS_ = (SL, 1024)
    WSC = sb("WSC", [128, 2], F32)
    MWACC = sb("MWACC", [128, 1], F32)
    WI8 = sb("WI8", [128, 2048], I8)

    SC = ps("SC", [128, 2048])
    CT0 = ps("CT0", [128, 512])
    CT1 = ps("CT1", [128, 512])
    SMS = ps("SMS", [128, 512])
    SP2 = ps("SP2", [128, 512])
    STAT = SP2

    class _View:
        def __init__(self, h, off):
            self.h, self.off = h, off
        def __getitem__(self, idx):
            p, f = idx
            return self.h[p, slice(self.off + (f.start or 0), self.off + f.stop)]
    FA = _View(SC, 0)
    FB = _View(SC, 512)

    semh = {}
    for s in ("sPE", "sACT", "sDVE", "sGP", "sSYNC",
              "dPE", "dACT", "dDVE", "dGP", "dSYNC", "ccKV", "ccKV2", "ccHAL"):
        semh[s] = st.enter_context(nc.semaphore(s))

    def OP(eng, fn, r=(), w=(), dma=False, cc=None, tag="", fence=False):
        sch.op(eng, fn, r=r, w=w, dma=dma, cc=cc, tag=tag, fence=fence)

    def DMA(eng, out_ap, in_ap, r=(), w=(), tag=""):
        OP(eng, lambda e, o=out_ap, i=in_ap: e.dma_start(out=o, in_=i),
           r=r, w=w, dma=True, tag=tag)

    def MM(out_ap, lhs_ap, rhs_ap, start, stop, tp=None, r=(), w=(), tag=""):
        OP("PE", lambda e, o=out_ap, l=lhs_ap, rr=rhs_ap, s1=start, s2=stop, t=tp:
           e.matmul(o, l, rr, start=s1, stop=s2, tile_position=t, skip_group_check=True),
           r=r, w=w, tag=tag)

    def RS(i, w=260):
        t = RWS[i // 4]
        b = 32 * (i % 4)
        return t[b:b + 1, 0:w]

    RKEY = [f"row{i}" for i in range(12)]

    DMA("SYNC", TRI[:, :], trid[:, :], w=["TRI"])
    DMA("SYNC", IDF[:, :], idfd[:, :], w=["IDF"])
    DMA("SYNC", SELR[:, :], selrowd[:, :], w=["SELR"])
    DMA("SYNC", BP[:, :], bpd[:, :], w=["BP"])
    DMA("SYNC", CW[:, :], convd[:, :], w=["CW"])
    DMA("SYNC", SELC[:, :], seld[:, :], w=["SELC"])
    for i in range(2):
        DMA("SYNC", EDG[i][:, :], edged[i, :, :], w=[f"EDG{i}"])
    OP("GP", lambda e: e.memset(ONESB[:, :], 1.0), w=["ONESB"])
    OP("GP", lambda e: e.memset(RECIP[:, :], 1.0), w=["recip"])
    OP("GP", lambda e: e.memset(ONESF[:, :], 1.0), w=["ONESF"])
    OP("GP", lambda e: e.memset(CONSTS[:, 0:1], EPS), w=["CONSTS"])
    OP("GP", lambda e: e.memset(CONSTS[:, 1:2], LOG127), w=["CONSTS"])
    OP("GP", lambda e: e.memset(ONE1F[:, :], 1.0), w=["ONE1F"])

    def wprep(wTd, wqdst, nslices, width, wcol, key):
        for s_ in range(nslices):
            DMA("SYNC", SL[:, 0:width], wTd[128 * s_:128 * (s_ + 1), :],
                w=["sl"], tag=f"wld_{key}_{s_}")
            OP("DVE", lambda e, w_=width: e.reduce_sum(TCOL[:, 0:1], SL[:, 0:w_],
               axis=AX.X, apply_absolute_value=True),
               r=["sl"], w=["tcol0"], fence=True)
            if s_ == 0:
                OP("DVE", lambda e: e.tensor_copy(MWACC[:, :], TCOL[:, 0:1]),
                   r=["tcol0"], w=["mwacc"])
            else:
                OP("DVE", lambda e: e.tensor_add(MWACC[:, :], MWACC[:, :], TCOL[:, 0:1]),
                   r=["tcol0", "mwacc"], w=["mwacc"])
        MM(STAT[0:1, 0:1], ONESF[:, :], MWACC[:, :], True, True,
           r=["mwacc", "ONESF"], w=["pSP2"], tag=f"wsum_{key}")
        n_el = nslices * 128 * width
        OP("ACT", lambda e, n=n_el: e.activation(RS(0)[:, 0:1], STAT[0:1, 0:1],
           AF.Copy, bias=0.0, scale=1.0 / n), r=["pSP2"], w=["row0"])
        OP("DVE", lambda e: e.tensor_scalar(out=RS(0)[:, 0:1], in0=RS(0)[:, 0:1],
           scalar1=1e-5, scalar2=None, op0=ALU.max), r=["row0"], w=["row0"])
        OP("DVE", lambda e: e.reciprocal(RS(4)[:, 0:1], RS(0)[:, 0:1]),
           r=["row0"], w=["row4"], fence=True)
        MM(SP2[:, 0:1], ONE1F[:, :], RS(4)[:, 0:1], True, True,
           r=["row4", "ONE1F"], w=["pSP2"])
        OP("ACT", lambda e, wc=wcol: e.copy(wc, SP2[:, 0:1]),
           r=["pSP2"], w=[key + "wsc"])
        cidx = 2 if key == "w1" else 3
        OP("DVE", lambda e, ci=cidx: e.tensor_scalar(out=CONSTS[0:1, ci:ci + 1],
           in0=RS(0)[:, 0:1], scalar1=1.0 / 127.0, scalar2=None, op0=ALU.mult),
           r=["row0"], w=["CONSTS"])
        for s_ in range(nslices):
            DMA("SYNC", SL[:, 0:width], wTd[128 * s_:128 * (s_ + 1), :],
                w=["sl"], tag=f"wld2_{key}_{s_}")
            for h0 in range(0, width, 2048):
                hw = min(2048, width - h0)
                OP("DVE", lambda e, wc=wcol, o=h0, hw_=hw: e.tensor_scalar(
                   out=SL[:, 4160:4160 + hw_],
                   in0=SL[:, o:o + hw_], scalar1=wc, scalar2=1.4995,
                   op0=ALU.mult, op1=ALU.min), r=["sl", key + "wsc"], w=["sl"])
                OP("DVE", lambda e, hw_=hw: e.tensor_scalar(out=WI8[:, 0:hw_],
                   in0=SL[:, 4160:4160 + hw_], scalar1=-1.4995, scalar2=None,
                   op0=ALU.max), r=["sl"], w=["wi8"])
                OP("DVE", lambda e, hw_=hw: e.tensor_copy(XQ[:, 0:hw_], WI8[:, 0:hw_]),
                   r=["wi8"], w=["xq"])
                DMA("SYNC", wqdst[128 * s_:128 * (s_ + 1), h0:h0 + hw],
                    XQ[:, 0:hw], r=["xq"], tag=f"wst_{key}_{s_}_{h0}")

    wprep(w1Td, w1qd, 8, C, WSC[:, 0:1], "w1")
    wprep(w2Td, w2qd, 32, D, WSC[:, 1:2], "w2")


    def bcast_build(row_ap, dst_bc, wdt, rkeys, bckey, tag=""):
        MM(SP2[:, 0:wdt], ONE1F[:, :], row_ap, True, True,
           r=list(rkeys) + ["ONE1F"], w=["pSP2"], tag="bc" + tag)
        OP("ACT", lambda e, d=dst_bc, w_=wdt: e.copy(d[:, 0:w_], SP2[:, 0:w_]),
           r=["pSP2"], w=[bckey], tag="bce" + tag)

    def dance(M_t, mkey, wdt, outrow_i, tag=""):
        OP("DVE", lambda e, w_=wdt, M=M_t: e.tensor_copy(ACC[0][0:64, 0:w_],
           M[64:128, 0:w_]), r=[mkey], w=["acc0"], tag="dc0" + tag)
        OP("DVE", lambda e, w_=wdt, M=M_t: e.tensor_tensor(out=M[0:64, 0:w_],
           in0=M[0:64, 0:w_], in1=ACC[0][0:64, 0:w_], op=ALU.max),
           r=[mkey, "acc0"], w=[mkey], tag="dc1" + tag)
        OP("DVE", lambda e, w_=wdt, M=M_t: e.tensor_copy(ACC[0][0:32, 0:w_],
           M[32:64, 0:w_]), r=[mkey], w=["acc0"], tag="dc0b" + tag)
        OP("DVE", lambda e, w_=wdt, M=M_t: e.tensor_tensor(out=M[0:32, 0:w_],
           in0=M[0:32, 0:w_], in1=ACC[0][0:32, 0:w_], op=ALU.max),
           r=[mkey, "acc0"], w=[mkey], tag="dc2" + tag)
        off = 0
        while off < wdt:
            ln = min(128, wdt - off)
            OP("PE", lambda e, o=off, l=ln, M=M_t: e.transpose(SP2[0:l, 0:32],
               M[0:32, o:o + l], IDF[0:32, 0:32]),
               r=[mkey, "IDF"], w=["pSP2"], tag="dct" + tag)
            OP("DVE", lambda e, l=ln: e.reduce_max(TCOL[0:l, 1:2], SP2[0:l, 0:32],
               axis=AX.X), r=["pSP2"], w=["tcol1"], tag="dcr" + tag, fence=True)
            OP("PE", lambda e, l=ln: e.transpose(SP2[0:1, 32:32 + l],
               TCOL[0:l, 1:2], IDF[0:l, 0:l]),
               r=["tcol1", "IDF"], w=["pSP2"], tag="dct2" + tag)
            OP("ACT", lambda e, o=off, l=ln, ri=outrow_i: e.copy(
               RS(ri)[:, o:o + l], SP2[0:1, 32:32 + l]),
               r=["pSP2"], w=[RKEY[outrow_i]], tag="dce" + tag)
            off += 128

    def ffn(t, Wd, call, fpar):
        xo = 0 if call == 1 else 1
        Wo = Wd - 2
        xk = f"X{t}"
        for ch in range(8):
            xs = XT[t][:, 260 * ch + xo: 260 * ch + xo + Wd]
            MM(STAT[64:65, 0:Wd], ONESF[:, :], xs, ch == 0, ch == 7, tp=(0, 64),
               r=[xk, "ONESF"], w=["pSP2"], tag=f"st{call}_{t}_{ch}")
            OP("ACT", lambda e, x_=xs, p=ch % 2, w_=Wd: e.activation(
               ACC[p][:, 0:w_], x_, AF.Square), r=[xk], w=[f"acc{ch % 2}"],
               tag=f"sq{call}_{t}_{ch}")
            MM(STAT[96:97, 0:Wd], ONESF[:, :], ACC[ch % 2][:, 0:Wd], ch == 0, ch == 7,
               tp=(0, 96), r=[f"acc{ch % 2}", "ONESF"], w=["pSP2"],
               tag=f"sx{call}_{t}_{ch}")
        OP("ACT", lambda e, w_=Wd: e.activation(RS(0)[:, 0:w_], STAT[64:65, 0:w_],
           AF.Copy, bias=0.0, scale=1.0 / 1024.0), r=["pSP2"], w=["row0"])
        OP("ACT", lambda e, w_=Wd: e.activation(RS(4)[:, 0:w_], STAT[96:97, 0:w_],
           AF.Copy, bias=0.0, scale=1.0 / 1024.0), r=["pSP2"], w=["row4"])
        OP("DVE", lambda e, w_=Wd: e.tensor_tensor(out=RS(8)[:, 0:w_],
           in0=RS(0)[:, 0:w_], in1=RS(0)[:, 0:w_], op=ALU.mult),
           r=["row0"], w=["row8"])
        OP("DVE", lambda e, w_=Wd: e.tensor_tensor(out=RS(4)[:, 0:w_],
           in0=RS(4)[:, 0:w_], in1=RS(8)[:, 0:w_], op=ALU.subtract),
           r=["row4", "row8"], w=["row4"])
        OP("ACT", lambda e, w_=Wd: e.activation(RS(4)[:, 0:w_], RS(4)[:, 0:w_],
           AF.Ln, bias=CONSTS[0:1, 0:1], scale=1.0), r=["row4", "CONSTS"], w=["row4"])
        OP("ACT", lambda e, w_=Wd: e.activation(RS(4)[:, 0:w_], RS(4)[:, 0:w_],
           AF.Exp, bias=0.0, scale=-0.5), r=["row4"], w=["row4"])
        bcast_build(RS(0)[:, 0:Wd], BC[0], Wd, ["row0"], "bc0", tag=f"mu{call}{t}")
        for ch in range(8):
            xs = XT[t][:, 260 * ch + xo: 260 * ch + xo + Wd]
            OP("DVE", lambda e, x_=xs, w_=Wd: e.tensor_tensor(
               out=ACC[2][:, 0:w_], in0=x_, in1=BC[0][:, 0:w_], op=ALU.subtract),
               r=[xk, "bc0"], w=["acc2"], tag=f"tt{call}_{t}_{ch}")
            if ch == 0:
                OP("DVE", lambda e, w_=Wd: e.tensor_scalar(
                   out=MH[:, 0:w_].bitcast(I32), in0=ACC[2][:, 0:w_].bitcast(I32),
                   scalar1=0x7FFFFFFF, scalar2=None, op0=ALU.bitwise_and),
                   r=["acc2"], w=["mh"])
            else:
                OP("DVE", lambda e, w_=Wd: e.tensor_scalar(
                   out=ACC[1][:, 0:w_].bitcast(I32), in0=ACC[2][:, 0:w_].bitcast(I32),
                   scalar1=0x7FFFFFFF, scalar2=None, op0=ALU.bitwise_and),
                   r=["acc2"], w=["acc1"])
                OP("DVE", lambda e, w_=Wd: e.tensor_tensor(out=MH[:, 0:w_],
                   in0=MH[:, 0:w_], in1=ACC[1][:, 0:w_], op=ALU.max),
                   r=["acc1", "mh"], w=["mh"])
        dance(MH, "mh", Wd, 0, tag=f"d{call}{t}")
        OP("DVE", lambda e, w_=Wd: e.tensor_tensor(out=RS(8)[:, 0:w_],
           in0=RS(4)[:, 0:w_], in1=RS(0)[:, 0:w_], op=ALU.mult),
           r=["row4", "row0"], w=["row8"])
        OP("DVE", lambda e, w_=Wd: e.tensor_scalar(out=RS(8)[:, 0:w_],
           in0=RS(8)[:, 0:w_], scalar1=1e-5, scalar2=None, op0=ALU.max),
           r=["row8"], w=["row8"])
        OP("ACT", lambda e, w_=Wd: e.activation(ACC[0][0:1, 0:w_], RS(8)[:, 0:w_],
           AF.Ln), r=["row8"], w=["acc0"])
        OP("ACT", lambda e, w_=Wd: e.activation(ACC[0][0:1, 0:w_], ACC[0][0:1, 0:w_],
           AF.Exp, bias=CONSTS[0:1, 1:2], scale=-1.0), r=["acc0", "CONSTS"], w=["acc0"])
        OP("DVE", lambda e, w_=Wd: e.tensor_tensor(out=ACC[0][0:1, 0:w_],
           in0=ACC[0][0:1, 0:w_], in1=RS(4)[:, 0:w_], op=ALU.mult),
           r=["acc0", "row4"], w=["acc0"])
        bcast_build(ACC[0][0:1, 0:Wd], BC[1], Wd, ["acc0"], "bc1", tag=f"kx{call}{t}")
        OP("DVE", lambda e, w_=Wd: e.tensor_scalar(out=RS(8)[:, 0:w_],
           in0=RS(8)[:, 0:w_], scalar1=CONSTS[0:1, 2:3], scalar2=None,
           op0=ALU.mult), r=["row8", "CONSTS"], w=["row8"], tag="beta")
        if call == 3:
            ei_, eb_ = (t // 4), 32 * (t % 4)
            OP("DVE", lambda e, i_=ei_, b_=eb_, w_=Wd: e.tensor_copy(
               ACC[0][0:1, 0:w_], EDG[i_][b_:b_ + 1, 0:w_]),
               r=[f"EDG{ei_}"], w=["acc0"])
            OP("DVE", lambda e, w_=Wd: e.tensor_tensor(
               out=RS(8)[:, 0:w_], in0=RS(8)[:, 0:w_],
               in1=ACC[0][0:1, 0:w_], op=ALU.mult),
               r=["row8", "acc0"], w=["row8"])
        bcast_build(RS(8)[:, 0:Wd], BC[2], Wd, ["row8"], "bc2", tag=f"bt{call}{t}")
        if DBG and call == 1 and t == 0:
            DMA("GP", dbg3[11, 0:1, 0:4], CONSTS[0:1, :], r=["CONSTS"], tag="dbg3_11")
            DMA("GP", dbg3[0, :, :], BC[0][:, 0:260], r=["bc0"], tag="dbg3_0")
            DMA("GP", dbg3[1, :, :], BC[1][:, 0:260], r=["bc1"], tag="dbg3_1")
            DMA("GP", dbg3[2, :, :], BC[2][:, 0:260], r=["bc2"], tag="dbg3_2")
        for ch in range(8):
            xs = XT[t][:, 260 * ch + xo: 260 * ch + xo + Wd]
            OP("DVE", lambda e, x_=xs, w_=Wd: e.tensor_tensor(out=ACC[2][:, 0:w_],
               in0=x_, in1=BC[0][:, 0:w_], op=ALU.subtract),
               r=[xk, "bc0"], w=["acc2"])
            OP("DVE", lambda e, w_=Wd: e.tensor_tensor(out=WI8[:, 0:w_],
               in0=ACC[2][:, 0:w_], in1=BC[1][:, 0:w_], op=ALU.mult),
               r=["acc2", "bc1"], w=["wi8"])
            OP("DVE", lambda e, ch_=ch, w_=Wd: e.tensor_copy(
               XQ[:, 260 * ch_: 260 * ch_ + w_], WI8[:, 0:w_]),
               r=["wi8"], w=["xq"], tag=f"xq{call}_{t}_{ch}")
        for cc in range(32):
            par = cc % 2
            DMA("SYNC", W1S[par][:, :].rearrange("p (k c) -> p k c", c=128),
                w1qd[:, 128 * cc:128 * (cc + 1)].rearrange("(k p) c -> p k c", p=128),
                r=[], w=[f"w1s{par}"], tag=f"w1l{call}_{t}_{cc}")
            PSM = FA if (fpar + cc) % 2 == 0 else FB
            pk = "pFA" if (fpar + cc) % 2 == 0 else "pFB"
            for k8 in range(8):
                MM(PSM[:, 0:Wd], W1S[par][:, 128 * k8:128 * (k8 + 1)],
                   XQ[:, 260 * k8: 260 * k8 + Wd], k8 == 0, k8 == 7,
                   r=[f"w1s{par}", "xq"], w=[pk, "pSC"], tag=f"m1{call}_{t}_{cc}_{k8}")
            hp_ = cc % 2
            OP("DVE", lambda e, P=PSM, h_=hp_, w_=Wd: e.tensor_tensor(
               out=H1S[h_][:, 0:w_], in0=P[:, 0:w_], in1=BC[2][:, 0:w_],
               op=ALU.mult), r=[pk, "bc2"], w=[f"h1s{hp_}"],
               tag=f"bp_{call}_{t}_{cc}")
            OP("ACT", lambda e, h_=hp_, c_=cc, w_=Wo: e.activation(
               ACC[0][:, 0:w_], H1S[h_][:, 1:1 + w_], AF.Copy,
               scale=CW[:, 32 + c_:33 + c_]),
               r=[f"h1s{hp_}", "CW"], w=["acc0"], tag=f"cv1_{call}_{t}_{cc}")
            OP("DVE", lambda e, h_=hp_, c_=cc, w_=Wo: e.scalar_tensor_tensor(
               out=ACC[1][:, 0:w_], in0=H1S[h_][:, 0:w_], scalar=CW[:, c_:c_ + 1],
               in1=ACC[0][:, 0:w_], op0=ALU.mult, op1=ALU.add),
               r=[f"h1s{hp_}", "CW", "acc0"], w=["acc1"], tag=f"cv2_{call}_{t}_{cc}")
            OP("DVE", lambda e, h_=hp_, c_=cc, w_=Wo: e.scalar_tensor_tensor(
               out=ACC[0][:, 0:w_], in0=H1S[h_][:, 2:2 + w_], scalar=CW[:, 64 + c_:65 + c_],
               in1=ACC[1][:, 0:w_], op0=ALU.mult, op1=ALU.add),
               r=[f"h1s{hp_}", "CW", "acc1"], w=["acc0"], tag=f"cv3_{call}_{t}_{cc}")
            if DBG and call == 1 and t == 0 and cc == 0:
                DMA("GP", dbg3[3, :, :], XQ[:, 0:260], r=["xq"], tag="dbg3_3")
                DMA("GP", dbg3[4, :, :], H1S[0][:, 0:260], r=["h1s0"], tag="dbg3_4")
                DMA("GP", dbg3[5, :, 0:258], ACC[0][:, 0:258], r=["acc0"], tag="dbg3_5")
            OP("ACT", lambda e, c_=cc, w_=Wo: e.activation(
               SL[:, 258 * c_: 258 * c_ + w_], ACC[0][:, 0:w_], AF.Silu,
               bias=CW[:, 96 + c_:97 + c_], scale=1.0),
               r=["acc0", "CW"], w=["sl"], tag=f"si_{call}_{t}_{cc}")
            if DBG and call == 1 and t == 0 and cc == 0:
                DMA("GP", dbg3[6, :, 0:258], SL[:, 0:258], r=["sl"], tag="dbg3_6")
            if cc == 0:
                OP("DVE", lambda e, c_=cc, w_=Wo: e.tensor_scalar(
                   out=MH[:, 0:w_].bitcast(I32),
                   in0=SL[:, 258 * c_:258 * c_ + w_].bitcast(I32),
                   scalar1=0x7FFFFFFF, scalar2=None, op0=ALU.bitwise_and),
                   r=["sl"], w=["mh"])
            else:
                OP("DVE", lambda e, c_=cc, w_=Wo: e.tensor_scalar(
                   out=ACC[1][:, 0:w_].bitcast(I32),
                   in0=SL[:, 258 * c_:258 * c_ + w_].bitcast(I32),
                   scalar1=0x7FFFFFFF, scalar2=None, op0=ALU.bitwise_and),
                   r=["sl"], w=["acc1"])
                OP("DVE", lambda e, w_=Wo: e.tensor_tensor(out=MH[:, 0:w_],
                   in0=MH[:, 0:w_], in1=ACC[1][:, 0:w_], op=ALU.max),
                   r=["acc1", "mh"], w=["mh"])
        dance(MH, "mh", Wo, 0, tag=f"d2{call}{t}")
        OP("DVE", lambda e, w_=Wo: e.tensor_scalar(out=RS(0)[:, 0:w_],
           in0=RS(0)[:, 0:w_], scalar1=1e-5, scalar2=None, op0=ALU.max),
           r=["row0"], w=["row0"])
        OP("ACT", lambda e, w_=Wo: e.activation(RS(4)[:, 0:w_], RS(0)[:, 0:w_],
           AF.Ln), r=["row0"], w=["row4"])
        OP("ACT", lambda e, w_=Wo: e.activation(RS(4)[:, 0:w_], RS(4)[:, 0:w_],
           AF.Exp, bias=CONSTS[0:1, 1:2], scale=-1.0), r=["row4", "CONSTS"], w=["row4"])
        bcast_build(RS(4)[:, 0:Wo], BC[1], Wo, ["row4"], "bc1", tag=f"s2{call}{t}")
        OP("DVE", lambda e, w_=Wo: e.tensor_scalar(out=RS(8)[:, 0:w_],
           in0=RS(0)[:, 0:w_], scalar1=CONSTS[0:1, 3:4], scalar2=0.5,
           op0=ALU.mult, op1=ALU.mult), r=["row0", "CONSTS"], w=["row8"])
        if DBG and call == 1 and t == 0:
            DMA("GP", dbg3[7, :, 0:258], BC[1][:, 0:258], r=["bc1"], tag="dbg3_7")
            DMA("GP", dbg3[8, 0:1, 0:258], RS(8)[:, 0:258], r=["row8"], tag="dbg3_8")
            DMA("GP", dbg3[10, 0:1, 0:258], RS(0)[:, 0:258], r=["row0"], tag="dbg3_10")
        tb = 0
        for cc in range(32):
            OP("DVE", lambda e, c_=cc, w_=Wo: e.tensor_tensor(out=WI8[:, 0:w_],
               in0=SL[:, 258 * c_:258 * c_ + w_], in1=BC[1][:, 0:w_], op=ALU.mult),
               r=["sl", "bc1"], w=["wi8"], tag=f"q2a_{call}_{t}_{cc}")
            OP("DVE", lambda e, c_=cc, tb_=tb, w_=Wo: e.tensor_copy(
               KH[tb_][:, 258 * c_:258 * c_ + w_], WI8[:, 0:w_]),
               r=["wi8"], w=[f"kh{tb}"], tag=f"q2b_{call}_{t}_{cc}")
            if DBG and call == 1 and t == 0 and cc == 0:
                DMA("GP", dbg3[9, :, 0:258], KH[0][:, 0:258], r=["kh0"], tag="dbg3_9")


    def ffn_mm2(t, Wd, call):
        xo = 0 if call == 1 else 1
        Wo = Wd - 2
        bcast_build(RS(8)[:, 0:Wo], BC[0], Wo, ["row8"], "bcg0", tag=f"g{call}{t}")
        for d2 in range(8):
            par = d2 % 2
            DMA("SYNC", W2S[par][:, :].rearrange("p (k c) -> p k c", c=128),
                w2qd[:, 128 * d2:128 * (d2 + 1)].rearrange("(k p) c -> p k c", p=128),
                w=[f"w2s{par}"], tag=f"w2l{call}_{t}_{d2}")
            PSM = FA if d2 % 2 == 0 else FB
            pk = "pFA" if d2 % 2 == 0 else "pFB"
            for cc in range(32):
                MM(PSM[:, 0:Wo], W2S[par][:, 128 * cc:128 * (cc + 1)],
                   KH[0][:, 258 * cc:258 * cc + Wo], cc == 0, cc == 31,
                   r=[f"w2s{par}", "kh0"], w=[pk, "pSC"], tag=f"m2{call}_{t}_{d2}_{cc}")
            OP("DVE", lambda e, P=PSM, w_=Wo: e.tensor_tensor(
               out=ACC[2][:, 0:w_], in0=P[:, 0:w_], in1=BC[0][:, 0:w_],
               op=ALU.mult), r=[pk, "bcg0"], w=["acc2"],
               tag=f"ev1_{call}_{t}_{d2}")
            xs = XT[t][:, 260 * d2 + xo + 1: 260 * d2 + xo + 1 + Wo]
            OP("DVE", lambda e, x_=xs, w_=Wo: e.tensor_tensor(
               out=x_, in0=x_, in1=ACC[2][:, 0:w_], op=ALU.add),
               r=["acc2", f"X{t}"], w=[f"X{t}"], tag=f"ev2_{call}_{t}_{d2}")

    for t in range(8):
        for ch in range(8):
            DMA("SYNC", XT[t][:, 260 * ch:260 * (ch + 1)], xd[t, 128 * ch:128 * (ch + 1), :],
                w=[f"X{t}"], tag=f"xl{t}_{ch}")
    for t in range(8 if PHASES >= 1 else 0):
        ffn(t, 260, 1, 0)
        ffn_mm2(t, 260, 1)
        for ch in range(8 if DBG else 0):
            DMA("GP", dbg1[t, 128 * ch:128 * (ch + 1), :],
                XT[t][:, 260 * ch:260 * (ch + 1)], r=[f"X{t}"], tag=f"dbg1_{t}_{ch}")

    def load_cast(dst_ap, src_ap, wdt, dkey, tag):
        DMA("GP", dst_ap, src_ap, w=[dkey], tag="lc" + tag)

    for k8 in range(8):
        load_cast(W2S[0][:, 512 * k8:512 * (k8 + 1)], wqd[128 * k8:128 * (k8 + 1), :],
                  512, "w2s0", f"wq{k8}")
        load_cast(W2S[1][:, 128 * k8:128 * (k8 + 1)], wkd[128 * k8:128 * (k8 + 1), :],
                  128, "w2s1", f"wk{k8}")
        load_cast(W2S[1][:, 1024 + 256 * k8:1024 + 256 * (k8 + 1)],
                  wvd[128 * k8:128 * (k8 + 1), :], 256, "w2s1", f"wv{k8}")

    for t in range(8 if PHASES >= 2 else 0):
        xk = f"X{t}"
        for ch in range(8):
            xs = XT[t][:, 260 * ch + 1: 260 * ch + 259]
            MM(STAT[64:65, 0:258], ONESF[:, :], xs, ch == 0, ch == 7, tp=(0, 64),
               r=[xk, "ONESF"], w=["pSP2"], tag=f"l2s{t}_{ch}")
            OP("ACT", lambda e, x_=xs, p=ch % 2: e.activation(
               ACC[p][:, 0:258], x_, AF.Square), r=[xk], w=[f"acc{ch % 2}"])
            MM(STAT[96:97, 0:258], ONESF[:, :], ACC[ch % 2][:, 0:258], ch == 0, ch == 7,
               tp=(0, 96), r=[f"acc{ch % 2}", "ONESF"], w=["pSP2"])
        OP("ACT", lambda e: e.activation(RS(0)[:, 0:258], STAT[64:65, 0:258],
           AF.Copy, bias=0.0, scale=1.0 / 1024.0), r=["pSP2"], w=["row0"])
        OP("ACT", lambda e: e.activation(RS(4)[:, 0:258], STAT[96:97, 0:258],
           AF.Copy, bias=0.0, scale=1.0 / 1024.0), r=["pSP2"], w=["row4"])
        OP("DVE", lambda e: e.tensor_tensor(out=RS(8)[:, 0:258], in0=RS(0)[:, 0:258],
           in1=RS(0)[:, 0:258], op=ALU.mult), r=["row0"], w=["row8"])
        OP("DVE", lambda e: e.tensor_tensor(out=RS(4)[:, 0:258], in0=RS(4)[:, 0:258],
           in1=RS(8)[:, 0:258], op=ALU.subtract), r=["row4", "row8"], w=["row4"])
        OP("ACT", lambda e: e.activation(RS(8)[:, 0:258], RS(4)[:, 0:258],
           AF.Ln, bias=CONSTS[0:1, 0:1], scale=1.0), r=["row4", "CONSTS"], w=["row8"])
        OP("ACT", lambda e: e.activation(RS(8)[:, 0:258], RS(8)[:, 0:258],
           AF.Exp, bias=0.0, scale=-0.5), r=["row8"], w=["row8"])
        bcast_build(RS(0)[:, 0:258], BC[0], 258, ["row0"], "bc0", tag=f"l2m{t}")
        bcast_build(RS(8)[:, 0:258], BC[1], 258, ["row8"], "bc1", tag=f"l2r{t}")
        for ch in range(8):
            xs = XT[t][:, 260 * ch + 1: 260 * ch + 259]
            OP("DVE", lambda e, x_=xs: e.tensor_tensor(out=ACC[2][:, 0:258],
               in0=x_, in1=BC[0][:, 0:258], op=ALU.subtract),
               r=[xk, "bc0"], w=["acc2"])
            OP("DVE", lambda e, ch_=ch: e.tensor_tensor(
               out=XQ[:, 258 * ch_:258 * (ch_ + 1)], in0=ACC[2][:, 0:258],
               in1=BC[1][:, 0:258], op=ALU.mult), r=["acc2", "bc1"], w=["xq"],
               tag=f"xh{t}_{ch}")
        for g in range(4):
            for k8 in range(8):
                MM(FA[:, 0:256], W2S[0][:, 512 * k8 + 128 * g: 512 * k8 + 128 * (g + 1)],
                   XQ[:, 258 * k8 + 1: 258 * k8 + 257], k8 == 0, k8 == 7,
                   r=["w2s0", "xq"], w=["pFA", "pSC"], tag=f"q{t}_{g}_{k8}")
            OP("ACT", lambda e, t_=t, g_=g: e.copy(QT[t_][:, 256 * g_:256 * (g_ + 1)],
               FA[:, 0:256]), r=["pFA"], w=[f"q{t}"], tag=f"qe{t}_{g}")
        for k8 in range(8):
            MM(FB[:, 0:256], W2S[1][:, 128 * k8:128 * (k8 + 1)],
               XQ[:, 258 * k8 + 1: 258 * k8 + 257], k8 == 0, k8 == 7,
               r=["w2s1", "xq"], w=["pFB", "pSC"], tag=f"k{t}_{k8}")
        OP("ACT", lambda e, t_=t: e.copy(KL[t_][:, :], FB[:, 0:256]),
           r=["pFB"], w=[f"kl{t}"], tag=f"ke{t}")
        for tb in range(2):
            for k8 in range(8):
                MM(FA[:, 0:256], XQ[:, 258 * k8 + 1 + 128 * tb: 258 * k8 + 1 + 128 * (tb + 1)],
                   W2S[1][:, 1024 + 256 * k8: 1024 + 256 * (k8 + 1)], k8 == 0, k8 == 7,
                   r=["w2s1", "xq"], w=["pFA", "pSC"], tag=f"v{t}_{tb}_{k8}")
            OP("ACT", lambda e, t_=t, tb_=tb: e.copy(VL[t_][:, 256 * tb_:256 * (tb_ + 1)],
               FA[:, 0:256]), r=["pFA"], w=[f"vl{t}"], tag=f"ve{t}_{tb}")
        DMA("GP", kcond[:, 256 * t:256 * (t + 1)], KL[t][:, :], r=[f"kl{t}"],
            tag=f"kc{t}")
        DMA("GP", vcond[256 * t:256 * t + 128, :], VL[t][:, 0:256], r=[f"vl{t}"],
            tag=f"vc{t}a")
        DMA("GP", vcond[256 * t + 128:256 * (t + 1), :], VL[t][:, 256:512], r=[f"vl{t}"],
            tag=f"vc{t}b")

    if DBG:
        DMA("GP", dbg5[0, :, :], QT[4][:, :], r=["q4"], tag="dbg5q")
        DMA("GP", dbg5[1, :, 0:256], KL[4][:, :], r=["kl4"], tag="dbg5k")
        DMA("GP", dbg5[2, :, 0:512], VL[4][:, :], r=["vl4"], tag="dbg5v")
        DMA("GP", dbg5[3, :, 0:258], XQ[:, 0:258], r=["xq"], tag="dbg5x")
    if PHASES >= 3:
      OP("GP", lambda e: e.collective_compute(
        "AllGather", ALU.bypass, replica_groups=[list(range(8))],
        ins=[kcond.ap().opt()], outs=[kgatd.ap().opt()]),
        r=[f"kc{t}" for t in range(8)],
        w=["kgat"], cc="ccKV", tag="agk")
      OP("GP", lambda e: e.collective_compute(
        "AllGather", ALU.bypass, replica_groups=[list(range(8))],
        ins=[vcond.ap().opt()], outs=[vgatd.ap().opt()]),
        r=[f"vc{t}a" for t in range(8)] + [f"vc{t}b" for t in range(8)],
        w=["vgat"], cc="ccKV2", tag="agv")

    ei = 0

    def cring_refill(i0):
        n = min(8, NENT - i0)
        for m in range(4):
            DMA("GP", CRING[32 * m:32 * m + 1, 0:256 * n],
                ctabd[m:m + 1, 256 * i0:256 * (i0 + n)],
                w=["cring"], tag=f"cr{i0}_{m}")

    def attn_iter(ent, epar, kh_handle, kcol, kv_v, kvkeys, first_ctx, last_ctx,
                  diag_sb=None):
        nonlocal ei
        k, g, hp = ent["k"], ent["g"], ent["hp"]
        t = k
        ring = ei % 8
        if ring == 0:
            cring_refill(ei)
        glocal = 0 if g in (3, 1) else 1
        CTX = CT0 if glocal == 0 else CT1
        ctxk = "pCT0" if glocal == 0 else "pCT1"
        wdt = 256
        for m in range(4 if 's' in APARTS else 0):
            MM(SC[:, 512 * m:512 * m + wdt],
               kh_handle[32 * m:32 * (m + 1), kcol:kcol + 128],
               QT[t][32 * m:32 * (m + 1), 256 * g: 256 * g + wdt],
               True, False, tp=(32 * m, 0),
               r=kvkeys + [f"q{t}"], w=["pSC"], tag=f"sc{ei}_{m}")
        for m in range(4 if 'c' in APARTS else 0):
            MM(SC[:, 512 * m:512 * m + wdt],
               CRING[32 * m:32 * m + 1, 256 * ring:256 * ring + 128],
               ONESB[32 * m:32 * m + 1, 0:wdt],
               False, False, tp=(32 * m, 0),
               r=["cring", "ONESB"], w=["pSC"], tag=f"cj{ei}_{m}")
        for m in range(4 if 'c' in APARTS else 0):
            MM(SC[:, 512 * m:512 * m + wdt],
               CRING[32 * m:32 * m + 1, 256 * ring + 128:256 * ring + 256],
               ONESB[32 * m:32 * m + 1, 0:wdt],
               False, True, tp=(32 * m, 0),
               r=["cring", "ONESB"], w=["pSC"], tag=f"cl{ei}_{m}")
        if 'e' not in APARTS:
            pass
        elif diag_sb == 1:
            OP("GP", lambda e, p=epar: e.memset(
               ET[p][:, :].rearrange("p (s f) -> p s f", s=4)[:, :, 0:128], 0.0),
               w=[f"e{epar}"], tag=f"ez{ei}")
            OP("ACT", lambda e, p=epar: e.activation(
               ET[p][:, :].rearrange("p (s f) -> p s f", s=4)[:, :, 128:256],
               SC[:, :].rearrange("p (s f) -> p s f", s=4)[:, :, 128:256],
               AF.Exp, bias=0.0, scale=1.0 / SQ32),
               r=["pSC"], w=[f"e{epar}"], tag=f"ex{ei}")
        else:
            OP("ACT", lambda e, p=epar: e.activation(
               ET[p][:, :].rearrange("p (s f) -> p s f", s=4)[:, :, 0:256],
               SC[:, :].rearrange("p (s f) -> p s f", s=4)[:, :, 0:256],
               AF.Exp, bias=0.0, scale=1.0 / SQ32),
               r=["pSC"], w=[f"e{epar}"], tag=f"ex{ei}")
        if diag_sb is not None and 'd' in APARTS:
            coff = 0 if diag_sb == 0 else 128
            OP("DVE", lambda e, p=epar, co=coff: e.tensor_tensor(
               out=ET[p][:, :].rearrange("p (s f) -> p s f", s=4)[:, :, co:co + 128],
               in0=ET[p][:, :].rearrange("p (s f) -> p s f", s=4)[:, :, co:co + 128],
               in1=TRI[:, None, :].broadcast_to([128, 4, 128]),
               op=ALU.mult), r=[f"e{epar}", "TRI"], w=[f"e{epar}"], tag=f"tm{ei}")
        for pr in range(2 if 'x' in APARTS else 0):
            for half in range(2):
                hm = 2 * pr + half
                MM(CTX[64 * half:64 * (half + 1), 256 * pr:256 * (pr + 1)],
                   kv_v[:, 64 * hm:64 * (hm + 1)],
                   ET[epar][:, 256 * hm:256 * hm + 256],
                   first_ctx, last_ctx, tp=(0, 64 * half),
                   r=kvkeys + [f"e{epar}"], w=[ctxk], tag=f"cx{ei}_{pr}_{half}")
        if DBG and ent["kind"] == "diag" and k == 4 and hp == 1 and g == 1 and ent.get("sb") == 0:
            OP("ACT", lambda e: e.copy(SL[:, 5696:5952], SC[:, 0:256]),
               r=["pSC"], w=["sl"], tag="dbg7a")
            OP("ACT", lambda e: e.copy(SL[:, 5952:6208], SC[:, 512:768]),
               r=["pSC"], w=["sl"], tag="dbg7b")
            DMA("GP", dbg4[0, :, 0:256], SL[:, 5696:5952], r=["sl"], tag="dbg7ad")
            DMA("GP", dbg4[1, :, 0:256], SL[:, 5952:6208], r=["sl"], tag="dbg7bd")
            OP("DVE", lambda e, p=epar: e.tensor_copy(SL[:, 6208:6720], ET[p][:, 0:512]),
               r=[f"e{epar}"], w=["sl"], tag="dbg7c")
            DMA("GP", dbg4[2, :, :], SL[:, 6208:6720], r=["sl"], tag="dbg7cd")
        if DBG and ent["kind"] == "diag" and k == 4 and hp == 1 and g == 1 and ent.get("sb") == 1:
            OP("DVE", lambda e, p=epar: e.tensor_copy(SL[:, 6720:7232], ET[p][:, 0:512]),
               r=[f"e{epar}"], w=["sl"], tag="dbg7e")
            DMA("GP", dbg4[1, :, 256:512], SL[:, 6720:6976], r=["sl"], tag="dbg7ed")
        for sh in range(2 if 'm' in APARTS else 0):
            base = 32 * (2 * glocal + sh)
            MM(SMS[base:base + 1, 0:512], ONESB[:, 0:1],
               ET[epar][:, 512 * sh:512 * (sh + 1)],
               first_ctx, last_ctx, tp=(0, base),
               r=[f"e{epar}", "ONESB"], w=["pSMS"], tag=f"sm{ei}_{sh}")
        ei += 1

    idx = 0
    hp_count = 0
    while idx < (NENT if PHASES >= 4 else 0):
        ent = ENTRIES[idx]
        k, hp = ent["k"], ent["hp"]
        t = k
        b_k = k // 4
        grp = []
        while idx < NENT and ENTRIES[idx]["k"] == k and ENTRIES[idx]["hp"] == hp:
            grp.append(ENTRIES[idx])
            idx += 1
        hp_count += 1
        if hp_count > NHP:
            continue
        packs = (3, 2) if hp == 0 else (1, 0)
        ng = {g: sum(1 for e2 in grp if e2["g"] == g) for g in packs}
        seen = {g: 0 for g in packs}
        kvpar = 0
        lastJ = None
        for ent2 in grp:
            g = ent2["g"]
            fc = seen[g] == 0
            lc = seen[g] == ng[g] - 1
            seen[g] += 1
            if ent2["kind"] == "far":
                J = ent2["J"]
                if J != lastJ:
                    kvpar ^= 1
                    pp2 = J // 2
                    r_ = owner(b_k, pp2)
                    ks_ = 4 * b_k + pp2 // 8
                    hh = J % 2
                    DMA("GP", KB[kvpar][:, :],
                        kgatd[128 * r_:128 * (r_ + 1),
                              256 * ks_ + 128 * hh:256 * ks_ + 128 * (hh + 1)],
                        r=["kgat"], w=[f"kb{kvpar}"], tag=f"kbl{k}_{hp}_{J}")
                    DMA("GP", VB[kvpar][:, :],
                        vgatd[2048 * r_ + 256 * ks_ + 128 * hh:
                              2048 * r_ + 256 * ks_ + 128 * (hh + 1), :],
                        r=["vgat"], w=[f"vb{kvpar}"], tag=f"vbl{k}_{hp}_{J}")
                    lastJ = J
                attn_iter(ent2, ei % 2, KB[kvpar], 0, VB[kvpar][:, :],
                          [f"kb{kvpar}", f"vb{kvpar}"], fc, lc)
            else:
                sb_ = ent2["sb"]
                attn_iter(ent2, ei % 2, KL[t], 128 * sb_,
                          VL[t][:, 256 * sb_:256 * (sb_ + 1)],
                          [f"kl{t}", f"vl{t}"], fc, lc, diag_sb=sb_)
        if 'u' not in APARTS:
            continue
        if DBG and k == 4:
            di = 0 if hp == 0 else 3
            OP("ACT", lambda e, d_=di: e.copy(SL[:, 4160:4672], CT0[:, :]),
               r=["pCT0"], w=["sl"], tag=f"dbg4a{hp}")
            DMA("GP", dbg4[di, :, :], SL[:, 4160:4672], r=["sl"], tag=f"dbg4ad{hp}")
            OP("ACT", lambda e, d_=di: e.copy(SL[:, 4672:5184], CT1[:, :]),
               r=["pCT1"], w=["sl"], tag=f"dbg4b{hp}")
            DMA("GP", dbg4[di + 1, :, :], SL[:, 4672:5184], r=["sl"], tag=f"dbg4bd{hp}")
            for ri_ in range(4):
                OP("ACT", lambda e, r2=ri_: e.copy(SL[32 * r2:32 * r2 + 1, 5184:5696],
                   SMS[32 * r2:32 * r2 + 1, :]), r=["pSMS"], w=["sl"], tag=f"dbg4c{hp}_{ri_}")
            DMA("GP", dbg4[di + 2, :, :], SL[:, 5184:5696], r=["sl"], tag=f"dbg4cd{hp}")
        for sh in range(4):
            OP("DVE", lambda e, b=32 * sh: e.tensor_copy(RECIP[b:b + 1, 0:512],
               SMS[b:b + 1, 0:512]), r=["pSMS"], w=["recip"], tag=f"se{k}_{hp}_{sh}")
        OP("DVE", lambda e: e.tensor_scalar(out=RECIP[:, :], in0=RECIP[:, :],
           scalar1=1e-30, scalar2=None, op0=ALU.add), r=["recip"], w=["recip"])
        OP("ACT", lambda e: e.activation(RECIP[:, :], RECIP[:, :], AF.Ln),
           r=["recip"], w=["recip"])
        OP("ACT", lambda e: e.activation(RECIP[:, :], RECIP[:, :], AF.Exp,
           bias=0.0, scale=-1.0), r=["recip"], w=["recip"])
        for glocal, g in enumerate(packs):
            CTX = CT0 if glocal == 0 else CT1
            ctxk = "pCT0" if glocal == 0 else "pCT1"
            for pr in range(2):
                rb = 32 * (2 * glocal + pr)
                rpar = pr % 2
                MM(SP2[:, 0:256], SELR[rb:rb + 1, 0:128], RECIP[rb:rb + 1, 0:256],
                   True, False, tp=(rb, 0), r=["recip", "SELR"], w=["pSP2"],
                   tag=f"r1{k}{hp}{g}{pr}")
                MM(SP2[:, 0:256], SELR[rb:rb + 1, 128:256], RECIP[rb:rb + 1, 256:512],
                   False, True, tp=(rb, 0), r=["recip", "SELR"], w=["pSP2"],
                   tag=f"r2{k}{hp}{g}{pr}")
                OP("ACT", lambda e, rp=rpar: e.copy(RSB[rp][:, :], SP2[:, 0:256]),
                   r=["pSP2"], w=[f"rsb{rpar}"], tag=f"re{k}{hp}{g}{pr}")
                d2 = 2 * g + pr
                OP("DVE", lambda e, rp=rpar, P=CTX, pr_=pr, d_=d2: e.tensor_tensor(
                   out=CTXN[:, 256 * d_:256 * (d_ + 1)],
                   in0=P[:, 256 * pr_:256 * (pr_ + 1)], in1=RSB[rp][:, :],
                   op=ALU.mult), r=[f"rsb{rpar}", ctxk], w=["ctxn"],
                   tag=f"cn{k}{hp}{g}{pr}")
        if DBG and hp == 1 and k == 4:
            DMA("GP", dbg4[6, :, :], CTXN[:, 512:1024], r=["ctxn"], tag="dbg4e")
            DMA("GP", dbg4[7, :, :], CTXN[:, 0:512], r=["ctxn"], tag="dbg4f")
        if hp == 1:
            for ec in range(8):
                par = ec % 2
                DMA("GP", WPS[par][:, :].rearrange("p (k c) -> p k c", c=128),
                    wpd[:, 128 * ec:128 * (ec + 1)].rearrange("(k p) c -> p k c", p=128),
                    w=[f"wps{par}"], tag=f"wpl{k}_{ec}")
                for d2 in range(8):
                    MM(SP2[:, 0:256], WPS[par][:, 128 * d2:128 * (d2 + 1)],
                       CTXN[:, 256 * d2:256 * (d2 + 1)], d2 == 0, d2 == 7,
                       r=[f"wps{par}", "ctxn"], w=["pSP2"], tag=f"pj{k}_{ec}_{d2}")
                xs = XT[t][:, 260 * ec + 2: 260 * ec + 258]
                OP("DVE", lambda e, x_=xs, ec_=ec: e.scalar_tensor_tensor(
                   out=x_, in0=SP2[:, 0:256], scalar=BP[:, ec_:ec_ + 1], in1=x_,
                   op0=ALU.add, op1=ALU.add), r=["pSP2", "BP", f"X{t}"],
                   w=[f"X{t}"], tag=f"pe{k}_{ec}")

    for ch in range(8 if (PHASES >= 4 and NHP >= 16) else 0):
        for t in range(8):
            OP("DVE", lambda e, t_=t, ch_=ch: e.tensor_copy(
               HSTG[:, 2 * t_:2 * t_ + 1], XT[t_][:, 260 * ch_ + 2:260 * ch_ + 3]),
               r=[f"X{t}"], w=["hstg"], tag=f"hs{ch}_{t}a")
            OP("DVE", lambda e, t_=t, ch_=ch: e.tensor_copy(
               HSTG[:, 2 * t_ + 1:2 * t_ + 2], XT[t_][:, 260 * ch_ + 257:260 * ch_ + 258]),
               r=[f"X{t}"], w=["hstg"], tag=f"hs{ch}_{t}b")
        OP("PE", lambda e: e.transpose(SP2[0:16, 0:128], HSTG[:, 0:16], IDF[:, :]),
           r=["hstg", "IDF"], w=["pSP2"], tag=f"ht{ch}")
        OP("ACT", lambda e, ch_=ch: e.copy(SL[0:16, 128 * ch_:128 * (ch_ + 1)],
           SP2[0:16, 0:128]), r=["pSP2"], w=["halt", "sl"], tag=f"he{ch}")
    if PHASES >= 4 and NHP >= 16:
      DMA("GP", halind[:, :], SL[0:16, 0:1024], r=["halt", "sl"], tag="hst")
      OP("GP", lambda e: e.collective_compute(
        "AllGather", ALU.bypass, replica_groups=[list(range(8))],
        ins=[halind.ap().opt()], outs=[halgd.ap().opt()]),
        r=["hst", "halt"], w=["halg"], cc="ccHAL", tag="aghal")
    if PHASES >= 4:
        OP("GP", lambda e: e.memset(SL[0:48, 1024:2048], 0.0),
           w=["cands", "sl"], tag="czero")
    for k in range(8 if PHASES >= 4 else 0):
        t = k
        nlo = max(0, 2 * k - 2)
        nhi = min(16, 2 * k + 4)
        nq = nhi - nlo
        for r_ in range(8):
            DMA("GP", SL[6 * r_:6 * r_ + nq, 1024:2048],
                halgd[16 * r_ + nlo:16 * r_ + nhi, :],
                r=["halg"], w=["cands", "sl"], tag=f"cl{k}_{r_}")
        for ch in range(8):
            MM(SP2[0:128, 0:2], SL[0:48, 1024 + 128 * ch:1024 + 128 * (ch + 1)],
               SELC[:, 2 * k:2 * (k + 1)], True, True,
               r=["cands", "sl"], w=["pSP2"], tag=f"hm{k}_{ch}")
            OP("ACT", lambda e, t_=t, ch_=ch: e.copy(
               XT[t_][:, 260 * ch_ + 1:260 * ch_ + 2], SP2[0:128, 0:1]),
               r=["pSP2"], w=[f"X{t}"], tag=f"hwl{k}_{ch}")
            OP("ACT", lambda e, t_=t, ch_=ch: e.copy(
               XT[t_][:, 260 * ch_ + 258:260 * ch_ + 259], SP2[0:128, 1:2]),
               r=["pSP2"], w=[f"X{t}"], tag=f"hwr{k}_{ch}")

    for t in range(8 if (DBG and PHASES >= 4 and NHP >= 16) else 0):
        for ch in range(8):
            DMA("GP", dbg2[t, 128 * ch:128 * (ch + 1), :],
                XT[t][:, 260 * ch:260 * (ch + 1)], r=[f"X{t}"], tag=f"dbg2_{t}_{ch}")
    for t in (range(7, -1, -1) if PHASES >= 5 else []):
        ffn(t, 258, 3, 0)
        ffn_mm2(t, 258, 3)
        for ch in range(8):
            DMA("GP", outd[t, 128 * ch:128 * (ch + 1), :],
                XT[t][:, 260 * ch + 2:260 * ch + 258],
                r=[f"X{t}"], tag=f"out{t}_{ch}")

    def final_wait(e):
        return e.engine_nop()
    OP("GP", final_wait, r=[f"X{t}" for t in range(8)], tag="fin")

    emis = sch.finalize()
    return nc, st, semh, emis, sch


def _emit(nc, st, semh, emis, sch):
    tot = defaultdict(int)
    for o in sch.ops:
        if "ms" in o and o["ms"][1] is not None:
            s, v, _ = o["ms"]
            tot[s] = max(tot[s], v)
    with nc.Block() as block:
        def mk(engname):
            def run(e):
                for ws, fn, ms, tag in emis[engname]:
                    for s, v in ws:
                        e.wait_ge(semh[s], v)
                    ins = fn(e)
                    if ms is not None:
                        s, v, step = ms
                        if v is None:
                            ins.then_inc(semh[s])
                        else:
                            ins.then_inc(semh[s], step)
                if engname == "GP":
                    for s in ("dGP", "dSYNC", "sPE", "sACT", "sDVE"):
                        if tot.get(s):
                            e.wait_ge(semh[s], tot[s])
            return run
        block.tensor(mk("PE"))
        block.scalar(mk("ACT"))
        block.vector(mk("DVE"))
        block.gpsimd(mk("GP"))
        block.sync(mk("SYNC"))
    st.close()
    return nc


def host_inputs(c, inputs):
    x = np.asarray(inputs["L0_attn_x"])
    xs = np.zeros((8, D, 260), dtype=np.float32)
    sel = np.zeros((48, 16), dtype=np.float32)
    edge = np.ones((2, 128, 258), dtype=np.float32)
    for k in range(8):
        b, pp = core_tile(c, k)
        S = TS * pp
        lo, hi = S - 2, S + 258
        slo, shi = max(lo, 0), min(hi, T)
        if shi > slo:
            xs[k][:, slo - lo: shi - lo] = np.ascontiguousarray(x[b, slo:shi, :].T)
        nlo = max(0, 2 * k - 2)
        if pp - 1 >= 0:
            r1 = owner(b, pp - 1)
            k1 = 4 * b + (pp - 1) // 8
            sel[6 * r1 + (2 * k1 + 1 - nlo), 2 * k] = 1.0
        if pp + 1 < NT:
            r2 = owner(b, pp + 1)
            k2 = 4 * b + (pp + 1) // 8
            sel[6 * r2 + (2 * k2 + 0 - nlo), 2 * k + 1] = 1.0
        ei_, eb_ = k // 4, 32 * (k % 4)
        if pp == 0:
            edge[ei_, eb_, 0] = 0.0
        if pp == NT - 1:
            edge[ei_, eb_, 257] = 0.0
    ctab = np.zeros((4, NENT * 256), dtype=bfloat16)
    p128 = np.arange(128, dtype=np.float64)
    for i, e in enumerate(ENTRIES):
        k, g = e["k"], e["g"]
        b, pp = core_tile(c, k)
        S = TS * pp
        A = S + 128.0
        for m in range(4):
            h = 4 * g + m
            sl_ = SLOPES[h]
            if e["kind"] == "far":
                J = e["J"]
                if J >= 2 * pp:
                    vals = np.full(128, -1e30)
                else:
                    vals = sl_ * (128.0 * J + p128 - A)
            else:
                vals = sl_ * (128.0 * e["sb"] + p128 - 128.0)
            hi = vals.astype(np.float32).astype(bfloat16)
            lo = (vals.astype(np.float32)
                  - hi.astype(np.float32)).astype(bfloat16)
            ctab[m, 256 * i:256 * i + 128] = hi
            ctab[m, 256 * i + 128:256 * (i + 1)] = lo
    return dict(x=xs, sel=sel, edge=edge, ctab=ctab)


def kernel(L0_attn_x, L0_attn_Wq, L0_attn_Wk, L0_attn_Wv, L0_attn_Wp, L0_attn_bp,
           L0_ffn_w1, L0_ffn_conv_w, L0_ffn_conv_b, L0_ffn_w2):
    x = np.ascontiguousarray(np.asarray(L0_attn_x, dtype=np.float32))
    w1T = np.ascontiguousarray(np.asarray(L0_ffn_w1, dtype=np.float32).T)
    w2T = np.ascontiguousarray(np.asarray(L0_ffn_w2, dtype=np.float32).T)
    cw = np.asarray(L0_ffn_conv_w, dtype=np.float32).reshape(C, 3)
    cb = np.asarray(L0_ffn_conv_b, dtype=np.float32)
    convw = np.zeros((128, 128), dtype=np.float32)
    for kk in range(3):
        convw[:, 32 * kk:32 * (kk + 1)] = cw[:, kk].reshape(32, 128).T
    convw[:, 96:128] = cb.reshape(32, 128).T
    bp = np.ascontiguousarray(np.asarray(L0_attn_bp, dtype=np.float32).reshape(8, 128).T)
    tri = np.triu(np.ones((128, 128), dtype=np.float32), 0).astype(bfloat16)
    identf = np.eye(128, dtype=np.float32)
    shared = dict(
        w1T=w1T, w2T=w2T, convw=convw, bp=bp,
        Wq=np.ascontiguousarray(np.asarray(L0_attn_Wq, dtype=np.float32)),
        Wk=np.ascontiguousarray(np.asarray(L0_attn_Wk, dtype=np.float32)),
        Wv=np.ascontiguousarray(np.asarray(L0_attn_Wv, dtype=np.float32)),
        Wp=np.ascontiguousarray(np.asarray(L0_attn_Wp, dtype=np.float32)),
        tri=tri, identf=identf,
    )
    sr = np.zeros((128, 256), np.float32)
    for rb_ in (0, 32, 64, 96):
        sr[rb_, 0:64] = 1.0
        sr[rb_, 192:256] = 1.0
    shared["selrow"] = sr

    nc, st, semh, emis, sch = build_program()
    nc = _emit(nc, st, semh, emis, sch)

    in_maps = []
    for c in range(8):
        m = dict(shared)
        m.update(host_inputs(c, dict(L0_attn_x=x)))
        in_maps.append(m)
    res = run_bass_kernel_spmd(nc, in_maps, core_ids=list(range(8)))

    out = np.empty((NB, T, D), dtype=np.float32)
    for c in range(8):
        xo = res.results[c]["xout"]
        for k in range(8):
            b, pp = core_tile(c, k)
            out[b, TS * pp:TS * (pp + 1), :] = xo[k].T
    return out
